# version 14
# baseline (speedup 1.0000x reference)
import contextlib
import numpy as np
import ml_dtypes

import concourse.bass as bass
import concourse.mybir as mybir
from concourse.bass_utils import run_bass_kernel_spmd
from concourse.alu_op_type import AluOpType

F32 = mybir.dt.float32
BF16 = mybir.dt.bfloat16
F8 = mybir.dt.float8e4
AF = mybir.ActivationFunctionType
AX = mybir.AxisListType
MPM = mybir.MatmulPerfMode

BIGNEG = -1.0e30
FP8MAX = 448.0
EPS_CLAMP = FP8MAX * 1e-6
DEBUG = False


class Cfg:
    def __init__(self, S=2048, D=2048, H=16, n_cores=8):
        self.S, self.D, self.H = S, D, H
        self.DH = 128
        assert D == H * self.DH
        self.J = 4 * D
        self.NT = S // 128
        self.ND = D // 128
        self.NJ = self.J // 128
        self.W = 512
        self.NTW = S // self.W
        self.NDW = D // self.W
        self.NG = H // 4
        self.n_cores = n_cores
        self.SCALE = float(self.DH) ** 0.5
        self.WSL = 2048
        ntile_fc = (self.J * self.D) // (128 * self.WSL)
        assert ntile_fc % n_cores == 0
        self.n_wsl_fc = ntile_fc // n_cores
        self.n_wsl_wp = ntile_fc // n_cores
        self.n_wsl = self.n_wsl_fc + self.n_wsl_wp
        self.n_wpq = (self.NDW * self.NJ * self.W) // self.WSL


class Sem:
    def __init__(self, es, nc, name):
        self.h = es.enter_context(nc.semaphore(name))
        self.n = 0


class Guard:
    def __init__(self):
        self.deps = []

    def set(self, *deps):
        self.deps = list(deps)

    def wait(self, P, eng):
        for sem, val in self.deps:
            P.wait(eng, sem, val)


class B:
    def __init__(self):
        self.ops = {e: [] for e in ("sync", "vector", "tensor", "scalar", "gpsimd")}

    def emit(self, eng, fn):
        self.ops[eng].append(fn)

    def wait(self, eng, sem, val=None):
        v = sem.n if val is None else val
        if v > 0:
            self.ops[eng].append(lambda e, s=sem, vv=v: e.wait_ge(s.h, vv))

    def dma(self, eng, sem, out, in_, slow=False):
        def f(e, out=out, in_=in_, slow=slow):
            e.dma_start(out=out, in_=in_,
                        allow_slow_non_contiguous=slow).then_inc(sem.h, 16)
        self.emit(eng, f)
        sem.n += 16

    def op(self, eng, sem, fn, amt=1):
        def f(e, fn=fn):
            fn(e).then_inc(sem.h, amt)
        self.emit(eng, f)
        sem.n += amt

    def opq(self, eng, fn):
        self.emit(eng, fn)

    def sw(self, eng, sem):
        self.wait(eng, sem, sem.n)


def build_nc(c: Cfg):
    nc = bass.Bass("TRN2", target_bir_lowering=False, debug=False,
                   detect_race_conditions=False)
    SLABW = c.ND * 128

    x_h = nc.dram_tensor("x", [c.S, c.D], F32, kind="ExternalInput").ap()
    wq_h = nc.dram_tensor("wqTb", [c.ND, 128, SLABW], F32, kind="ExternalInput").ap()
    wk_h = nc.dram_tensor("wkTb", [c.ND, 128, SLABW], F32, kind="ExternalInput").ap()
    wv_h = nc.dram_tensor("wvTb", [c.NG, 128, c.ND * 512], F32, kind="ExternalInput").ap()
    wo_h = nc.dram_tensor("woT", [c.D, c.D], F32, kind="ExternalInput").ap()
    wfc_h = nc.dram_tensor("wfcTb", [c.NJ, 128, SLABW], F32, kind="ExternalInput").ap()
    wp_h = nc.dram_tensor("wpTb", [c.NDW, 128, c.NJ * c.W], F32, kind="ExternalInput").ap()
    wsl_h = nc.dram_tensor("wslice", [c.n_wsl, 128, c.WSL], F32, kind="ExternalInput").ap()
    idb_h = nc.dram_tensor("identb", [128, 128], BF16, kind="ExternalInput").ap()
    idf_h = nc.dram_tensor("identf", [128, 128], F32, kind="ExternalInput").ap()
    ones_h = nc.dram_tensor("onesb", [1, 128], BF16, kind="ExternalInput").ap()
    mk1_h = nc.dram_tensor("mask1", [128, 1024], BF16, kind="ExternalInput").ap()
    mk2_h = nc.dram_tensor("mask2", [128, 1024], BF16, kind="ExternalInput").ap()

    out_h = nc.dram_tensor("out", [c.S, c.D], F32, kind="ExternalOutput").ap()

    ctx_s = nc.dram_tensor("ctx_scr", [c.ND, 128, c.S], BF16).ap()
    a_s = nc.dram_tensor("a_scr", [c.J, c.S], BF16).ap()
    a8_s = nc.dram_tensor("a8_scr", [c.NTW, 128, c.NJ * c.W], F8).ap()
    wp8_s = nc.dram_tensor("wp8_scr", [c.NDW, 128, c.NJ * c.W], F8).ap()
    wfc8_s = nc.dram_tensor("wfc8_scr", [c.NJ, 128, c.ND * 128], F8).ap()
    wq16_s = nc.dram_tensor("wq16_scr", [c.ND, 128, c.ND * 128], BF16).ap()
    wk16_s = nc.dram_tensor("wk16_scr", [c.ND, 128, c.ND * 128], BF16).ap()
    wv16_s = nc.dram_tensor("wv16_scr", [c.NG, 128, c.ND * 512], BF16).ap()
    wo16_s = nc.dram_tensor("wo16_scr", [c.D, c.D], BF16).ap()
    r1_s = nc.dram_tensor("r1_scr", [c.S, c.D], F32).ap()
    cc1_i = nc.dram_tensor("cc1_in", [4, 8], F32).ap()
    cc1_o = nc.dram_tensor("cc1_out", [4, 8], F32).ap()
    cc2_i = nc.dram_tensor("cc2_in", [4, 8], F32).ap()
    cc2_o = nc.dram_tensor("cc2_out", [4, 8], F32).ap()
    if DEBUG:
        dbg_xn = nc.dram_tensor("dbg_xn", [128, c.S], BF16, kind="ExternalOutput").ap()
        dbg_t4 = nc.dram_tensor("dbg_t4", [128, 1], F32, kind="ExternalOutput").ap()
        dbg_scb1 = nc.dram_tensor("dbg_scb1", [128, 4], F32, kind="ExternalOutput").ap()
        dbg_q = nc.dram_tensor("dbg_q", [128, c.S], BF16, kind="ExternalOutput").ap()
        dbg_k = nc.dram_tensor("dbg_k", [128, c.S], BF16, kind="ExternalOutput").ap()
        dbg_v = nc.dram_tensor("dbg_v", [128, 528], BF16, kind="ExternalOutput").ap()
        dbg_mrow = nc.dram_tensor("dbg_mrow", [1, c.S], BF16, kind="ExternalOutput").ap()
        dbg_ctx = nc.dram_tensor("dbg_ctx", [128, c.S], BF16, kind="ExternalOutput").ap()
        dbg_a = nc.dram_tensor("dbg_a", [128, c.S], BF16, kind="ExternalOutput").ap()
        dbg_hm = nc.dram_tensor("dbg_hm", [128, 2], F32, kind="ExternalOutput").ap()
        dbg_scb2 = nc.dram_tensor("dbg_scb2", [128, 12], F32, kind="ExternalOutput").ap()
        dbg_r1 = nc.dram_tensor("dbg_r1", [c.S, c.D], F32, kind="ExternalOutput").ap()
        dbg_as = nc.dram_tensor("dbg_as", [c.J, c.S], BF16, kind="ExternalOutput").ap()
        dbg_cs = nc.dram_tensor("dbg_cs", [c.ND, 128, c.S], BF16, kind="ExternalOutput").ap()
        dbg_wp8 = nc.dram_tensor("dbg_wp8", [c.NDW, 128, c.NJ * c.W], F8, kind="ExternalOutput").ap()
        dbg_a8 = nc.dram_tensor("dbg_a8", [c.NTW, 128, c.NJ * c.W], F8, kind="ExternalOutput").ap()
        dbg_ps = nc.dram_tensor("dbg_ps", [128, 512], F32, kind="ExternalOutput").ap()
        dbg_wpsl = nc.dram_tensor("dbg_wpsl", [128, c.NJ * c.W], F8, kind="ExternalOutput").ap()
        dbg_xs = nc.dram_tensor("dbg_xs", [128, 512], F32, kind="ExternalOutput").ap()

    es = contextlib.ExitStack()
    with es:
        sbuf = lambda n, s, d: es.enter_context(nc.sbuf_tensor(n, s, d))
        psumt = lambda n, s, d=F32: es.enter_context(nc.psum_tensor(n, s, d))

        A = sbuf("arenaA", [128, 32 * 1024], BF16)
        xn = [A[:, i * c.S:(i + 1) * c.S] for i in range(c.ND)]
        wob = [A[:, i * c.D:(i + 1) * c.D] for i in range(c.ND)]

        Bar = sbuf("arenaB", [128, 32 * 1024], BF16)
        qg = [Bar[:, i * c.S:(i + 1) * c.S] for i in range(4)]
        kg = [Bar[:, (4 + i) * c.S:(5 + i) * c.S] for i in range(4)]
        VCOL = 132
        vg = Bar[:, 8 * c.S: 8 * c.S + c.NT * 4 * VCOL]
        EOFF = 8 * c.S + c.NT * 4 * VCOL
        Ebuf = [Bar[:, EOFF + i * c.W: EOFF + (i + 1) * c.W] for i in range(2)]
        Bar8 = Bar.bitcast(F8)
        x8 = [Bar8[:, i * c.S:(i + 1) * c.S] for i in range(c.ND)]
        NH = c.ND // 2
        ctxh = [Bar[:, (8 + i) * c.S:(9 + i) * c.S] for i in range(NH)]
        a8sl = [Bar8[:, 0: c.NJ * c.W], Bar8[:, 16 * c.S: 16 * c.S + c.NJ * c.W]]

        Car = sbuf("arenaC", [128, 6 * 1024], F32)
        wf = [Car[:, i * 2048:(i + 1) * 2048] for i in range(2)]
        xt2 = [Car[:, 4096: 4096 + 2048], None]
        xs = [Car[:, 4096 + i * c.W: 4096 + (i + 1) * c.W] for i in range(2)]
        r1t = [Car[:, 5120 + i * c.W: 5120 + (i + 1) * c.W] for i in range(2)]
        CarB = Car.bitcast(BF16)
        abf = [CarB[:, i * 2048: (i + 1) * 2048] for i in range(2)]

        Dar = sbuf("arenaD", [128, 20 * 1024], BF16)
        wb = [Dar[:, i * 2048:(i + 1) * 2048] for i in range(4)]
        wvb = Dar[:, 2 * 2048: 2 * 2048 + c.ND * 512]
        DarF = Dar.bitcast(F32)
        sqD = DarF[:, 6144: 6144 + 2048]
        xt2[1] = DarF[:, 4096: 4096 + 2048]
        evb = [Dar[:, 2 * 2048 + i * c.W: 2 * 2048 + (i + 1) * c.W] for i in range(4)]
        abuf = [Dar[:, 2 * 2048 + (4 + i) * c.W: 2 * 2048 + (5 + i) * c.W] for i in range(4)]
        Dar8 = Dar.bitcast(F8)
        w8b = [Dar8[:, 16 * 2048 + i * 2048: 16 * 2048 + (i + 1) * 2048] for i in range(2)]
        a8t = [Dar8[:, 18 * 2048 + i * 2048: 18 * 2048 + (i + 1) * 2048] for i in range(2)]
        outt = [DarF[:, 8192 + i * c.W: 8192 + (i + 1) * c.W] for i in range(2)]
        wp8sl = Dar8[:, 0: c.NJ * c.W]

        mk1 = sbuf("mk1", [128, 1024], BF16)
        mk2 = sbuf("mk2", [128, 1024], BF16)
        idb = sbuf("idb", [128, 128], BF16)
        idf = sbuf("idf", [128, 128], F32)
        ones1 = sbuf("ones1", [1, 128], BF16)
        stat = sbuf("stat", [128, 16], F32)
        xmax = sbuf("xmax", [128, 1], F32)
        wmax = sbuf("wmax", [128, 2], F32)
        hmax = sbuf("hmax", [128, 1], F32)
        mcols = sbuf("mcols", [128, 4], F32)
        tmp4 = sbuf("tmp4", [128, 1], F32)
        zrow = sbuf("zrow", [4, 8], F32)
        m_sb = sbuf("m_sb", [128, 4], F32)
        mneg = sbuf("mneg", [128, 1], F32)
        mnegb = sbuf("mnegb", [128, 1], BF16)
        mnegs = sbuf("mnegs", [128, 16], F32)
        mnegb16 = sbuf("mnegb16", [128, 16], BF16)
        mrow = CarB[0:1, 8192: 8192 + c.S]
        ctv_head = CarB[:, 10240: 10240 + c.S]
        xtok = [Dar[:, 16384 + i * c.D: 16384 + (i + 1) * c.D] for i in range(2)]
        ctv = [sbuf(f"ctv{i}", [128, 128], BF16) for i in range(2)]
        ctv2 = [sbuf(f"ctv2{i}", [128, 128], BF16) for i in range(2)]
        scb = sbuf("scb", [128, 12], F32)
        epsc = sbuf("epsc", [128, 1], F32)

        ps_w = [psumt(f"ps_w{i}", [128, c.W]) for i in range(4)]
        ps_s = [psumt(f"ps_s{i}", [128, c.W]) for i in range(2)]
        PSF = ps_w + ps_s
        ps_t = [psumt(f"ps_t{i}", [128, 1024], BF16) for i in range(2)]

        LD = Sem(es, nc, "LD")
        GD = Sem(es, nc, "GD")
        TS = Sem(es, nc, "TS")
        VS = Sem(es, nc, "VS")
        SS = Sem(es, nc, "SS")
        CC = Sem(es, nc, "CC")
        ST = Sem(es, nc, "ST")
        CONSTS = Sem(es, nc, "CONSTS")
        XT = Sem(es, nc, "XT")
        WF = [Sem(es, nc, f"WF{i}") for i in range(2)]
        XSS = [Sem(es, nc, f"XSS{i}") for i in range(2)]
        ABFS = [Sem(es, nc, f"ABFS{i}") for i in range(2)]
        WP8L = Sem(es, nc, "WP8L")
        CTXL = Sem(es, nc, "CTXL")
        A8L = Sem(es, nc, "A8L")
        CTV2S = [Sem(es, nc, f"CTV2S{i}") for i in range(2)]
        R1S = [Sem(es, nc, f"R1S{i}") for i in range(2)]
        ABS = [Sem(es, nc, f"ABS{i}") for i in range(4)]
        A8S = [Sem(es, nc, f"A8S{i}") for i in range(2)]
        OUTS = [Sem(es, nc, f"OUTS{i}") for i in range(2)]
        BCS = Sem(es, nc, "BCS")
        W8L = [Sem(es, nc, f"W8L{i}") for i in range(2)]
        WCS = [Sem(es, nc, f"WCS{i}") for i in range(4)]
        g_wcv = [Guard() for _ in range(4)]
        WVL = Sem(es, nc, "WVL")
        QKL = [Sem(es, nc, f"QKL{i}") for i in range(2)]
        g_wvb = Guard()
        blk = es.enter_context(nc.Block())

        P = B()

        g_wf = [Guard(), Guard()]
        g_wb = [Guard(), Guard()]
        g_w8b = [Guard(), Guard()]
        g_xt = Guard()
        g_xtok = [Guard(), Guard()]
        g_psw = [Guard() for _ in range(4)]
        g_pss = [Guard(), Guard()]
        g_psf = g_psw + g_pss
        psrot = [0]
        def pick_bank():
            b = psrot[0] % 6
            psrot[0] += 1
            return b
        g_pst = [Guard(), Guard()]
        g_E = [Guard(), Guard()]
        g_ctv = [Guard(), Guard()]
        g_ctv2 = [Guard(), Guard()]
        g_evb = [Guard() for _ in range(4)]
        g_abuf = [Guard() for _ in range(4)]
        g_abf = [Guard(), Guard()]
        g_a8t = [Guard(), Guard()]
        g_xs = [Guard(), Guard()]
        g_mnegb = Guard()
        g_r1t = [Guard(), Guard()]
        g_outt = [Guard(), Guard()]

        P.dma("sync", CONSTS, idb[:, :], idb_h[:, :])
        P.dma("sync", CONSTS, idf[:, :], idf_h[:, :])
        P.dma("sync", CONSTS, ones1[:, :], ones_h[:, :])
        P.dma("sync", CONSTS, mk1[:, :], mk1_h[:, :])
        P.dma("sync", CONSTS, mk2[:, :], mk2_h[:, :])
        P.wait("tensor", CONSTS, CONSTS.n)
        P.opq("vector", lambda e: e.memset(xmax[:, :], 0.0))
        P.opq("vector", lambda e: e.memset(wmax[:, :], 0.0))
        P.opq("vector", lambda e: e.memset(hmax[:, :], 0.0))
        P.opq("vector", lambda e: e.memset(zrow[:, :], 0.0))
        P.opq("vector", lambda e: e.memset(mcols[:, :], 0.0))
        P.op("vector", VS, lambda e: e.memset(epsc[:, :], 1e-5))

        wcv_jobs = []
        for dc in range(c.ND):
            wcv_jobs.append((wq_h[dc, :, :], wq16_s[dc, :, :], SLABW, c.SCALE))
            wcv_jobs.append((wk_h[dc, :, :], wk16_s[dc, :, :], SLABW, 1.0))
        for g2 in range(c.NG):
            for q2 in range(c.ND * 512 // 2048):
                wcv_jobs.append((wv_h[g2, :, 2048 * q2:2048 * (q2 + 1)],
                                 wv16_s[g2, :, 2048 * q2:2048 * (q2 + 1)], 2048, 1.0))
        for dc in range(c.ND):
            wcv_jobs.append((wo_h[128 * dc:128 * (dc + 1), :],
                             wo16_s[128 * dc:128 * (dc + 1), :], c.D, 1.0))

        def emit_cv(k2):
            srcw, dstw, wdt, fold = wcv_jobs[k2]
            par = k2 % 2
            wpar = k2 % 4
            g_wf[par].wait(P, "sync")
            P.dma("sync", WF[par], wf[par][:, 0:wdt], srcw)
            eng = "scalar" if k2 % 2 == 0 else "vector"
            sem = SS if eng == "scalar" else VS
            P.wait(eng, WF[par])
            g_wcv[wpar].wait(P, eng)
            if eng == "scalar":
                P.op(eng, sem, lambda e, wpar=wpar, wdt=wdt, fold=fold: e.activation(
                    wb[wpar][:, 0:wdt], wf[wpar % 2][:, 0:wdt], AF.Copy, scale=fold))
            else:
                if fold == 1.0:
                    P.op(eng, sem, lambda e, wpar=wpar, wdt=wdt: e.tensor_copy(
                        wb[wpar][:, 0:wdt], wf[wpar % 2][:, 0:wdt]))
                else:
                    P.op(eng, sem, lambda e, wpar=wpar, wdt=wdt, fold=fold: e.tensor_scalar(
                        wb[wpar][:, 0:wdt], wf[wpar % 2][:, 0:wdt], fold, None,
                        AluOpType.mult))
            g_wf[par].set((sem, sem.n))
            P.wait("gpsimd", sem)
            P.dma("gpsimd", WCS[wpar], dstw, wb[wpar][:, 0:wdt])
            g_wcv[wpar].set((WCS[wpar], WCS[wpar].n))

        def emit_wsl(k):
            par = k % 2
            g_wf[par].wait(P, "sync")
            P.dma("sync", WF[par], wf[par][:, 0:c.WSL], wsl_h[k, :, :])
            P.wait("vector", WF[par])
            col = 0 if k < c.n_wsl_fc else 1
            P.op("vector", VS, lambda e, par=par: e.reduce_max(
                stat[:, 5:6], wf[par][:, 0:c.WSL], axis=AX.X,
                apply_absolute_value=True))
            P.sw("vector", VS)
            P.op("vector", VS, lambda e, col=col: e.tensor_tensor(
                wmax[:, col:col + 1], wmax[:, col:col + 1], stat[:, 5:6],
                AluOpType.max))
            g_wf[par].set((VS, VS.n))

        P.sw("vector", VS)

        g_xt2 = [Guard(), Guard()]
        for i in range(c.NT):
            par = i % 2
            o = 8 * par
            xt = xt2[par]
            g_xt2[par].wait(P, "sync")
            P.dma("sync", XT, xt[:, 0:c.D], x_h[128 * i:128 * (i + 1), :])
            P.wait("vector", XT)
            P.op("vector", VS, lambda e, o=o, xt=xt: e.reduce_sum(
                stat[:, o + 0:o + 1], xt[:, 0:c.D], axis=AX.X))
            P.sw("vector", VS)
            P.op("vector", VS, lambda e, o=o: e.tensor_scalar(
                stat[:, o + 1:o + 2], stat[:, o + 0:o + 1], -1.0 / c.D, None,
                AluOpType.mult))
            P.wait("scalar", VS)
            P.op("scalar", SS, lambda e, o=o, xt=xt, par=par: e.activation(
                sqD[:, 0:c.D] if par == 0 else sqD[:, 0:c.D], xt[:, 0:c.D],
                AF.Square, bias=stat[:, o + 1:o + 2],
                scale=1.0, accum_out=stat[:, o + 2:o + 3]))
            P.sw("scalar", SS)
            P.op("scalar", SS, lambda e, o=o: e.activation(
                stat[:, o + 3:o + 4], stat[:, o + 2:o + 3], AF.Sqrt,
                bias=epsc[:, 0:1], scale=1.0 / c.D))
            P.wait("vector", SS)
            P.op("vector", VS, lambda e, o=o: e.reciprocal(
                stat[:, o + 4:o + 5], stat[:, o + 3:o + 4]))
            P.sw("vector", VS)
            g_xtok[par].wait(P, "vector")
            P.op("vector", VS, lambda e, par=par, o=o, xt=xt: e.tensor_scalar(
                xtok[par][:, :], xt[:, 0:c.D], stat[:, o + 1:o + 2],
                stat[:, o + 4:o + 5], AluOpType.add, AluOpType.mult))
            g_xt2[par].set((VS, VS.n))
            for k2 in range(4 * i, min(4 * i + 4, len(wcv_jobs))):
                emit_cv(k2)
            if i < c.n_wsl:
                emit_wsl(i)
            P.op("vector", VS, lambda e, par=par: e.reduce_max(
                stat[:, 5:6], xtok[par][:, :], axis=AX.X,
                apply_absolute_value=True))
            P.sw("vector", VS)
            P.op("vector", VS, lambda e: e.tensor_tensor(
                xmax[:, :], xmax[:, :], stat[:, 5:6], AluOpType.max))
            P.wait("tensor", VS)
            for dc in range(c.ND):
                tp = dc % 2
                g_pst[tp].wait(P, "tensor")
                P.op("tensor", TS, lambda e, par=par, dc=dc, tp=tp: e.transpose(
                    ps_t[tp][:, 0:128], xtok[par][:, 128 * dc:128 * (dc + 1)],
                    idb[:, :]))
                P.wait("scalar", TS)
                P.op("scalar", SS, lambda e, i=i, dc=dc, tp=tp: e.copy(
                    xn[dc][:, 128 * i:128 * (i + 1)], ps_t[tp][:, 0:128]))
                g_pst[tp].set((SS, SS.n))
            g_xtok[par].set((TS, TS.n))
        for k2 in range(4 * c.NT, len(wcv_jobs)):
            emit_cv(k2)
        for k in range(c.NT, c.n_wsl):
            emit_wsl(k)
        ss_xn_done = SS.n

        P.op("vector", VS, lambda e: e.tensor_copy(mcols[:, 0:1], xmax[:, :]))
        P.op("vector", VS, lambda e: e.tensor_copy(mcols[:, 1:3], wmax[:, :]))
        P.wait("tensor", VS)
        g_psf[4].wait(P, "tensor")
        P.op("tensor", TS, lambda e: e.transpose(
            ps_s[0][0:4, 0:128], mcols[:, 0:4], idf[:, :]))
        P.wait("vector", TS)
        P.op("vector", VS, lambda e: e.reduce_max(
            tmp4[0:4, 0:1], ps_s[0][0:4, 0:128], axis=AX.X))
        g_psf[4].set((VS, VS.n))
        vs_cc1 = VS.n

        if DEBUG:
            P.wait("gpsimd", SS, ss_xn_done)
            P.dma("gpsimd", GD, dbg_xn[:, :], xn[0][:, :])
        P.wait("gpsimd", VS, vs_cc1)
        if DEBUG:
            P.dma("gpsimd", GD, dbg_t4[:, :], tmp4[:, :])
        P.dma("gpsimd", BCS, cc1_i[:, :], zrow[:, :])
        P.wait("gpsimd", BCS)
        P.dma("gpsimd", BCS, cc1_i[0:4, 0:1], tmp4[0:4, 0:1], slow=True)
        P.wait("gpsimd", BCS)
        P.op("gpsimd", CC, lambda e: e.collective_compute(
            "AllReduce", AluOpType.max,
            replica_groups=[list(range(c.n_cores))],
            ins=[cc1_i.opt()], outs=[cc1_o.opt()]))
        P.wait("gpsimd", CC)
        for j in range(3):
            P.dma("gpsimd", BCS, scb[:, j:j + 1],
                  cc1_o[j:j + 1, 0:1].partition_broadcast(128))
        bcs_bcast1 = BCS.n
        if DEBUG:
            P.wait("gpsimd", BCS, bcs_bcast1)
            P.dma("gpsimd", GD, dbg_scb1[:, :], scb[:, 0:4])

        P.wait("vector", BCS, bcs_bcast1)
        for (src, dstc) in ((0, 4), (1, 8), (2, 9)):
            P.op("vector", VS, lambda e, src=src: e.tensor_scalar(
                scb[:, 10:11], scb[:, src:src + 1], EPS_CLAMP, None, AluOpType.max))
            P.sw("vector", VS)
            P.op("vector", VS, lambda e: e.reciprocal(scb[:, 11:12], scb[:, 10:11]))
            P.sw("vector", VS)
            P.op("vector", VS, lambda e, dstc=dstc: e.tensor_scalar(
                scb[:, dstc:dstc + 1], scb[:, 11:12], FP8MAX / 2.0, None,
                AluOpType.mult))
        P.sw("vector", VS)
        P.op("vector", VS, lambda e: e.tensor_scalar(
            scb[:, 10:11], scb[:, 0:1], EPS_CLAMP, None, AluOpType.max))
        P.op("vector", VS, lambda e: e.tensor_scalar(
            scb[:, 11:12], scb[:, 1:2], EPS_CLAMP, None, AluOpType.max))
        P.sw("vector", VS)
        P.op("vector", VS, lambda e: e.tensor_tensor(
            scb[:, 5:6], scb[:, 10:11], scb[:, 11:12], AluOpType.mult))
        P.sw("vector", VS)
        P.op("vector", VS, lambda e: e.tensor_scalar(
            scb[:, 5:6], scb[:, 5:6], 4.0 / (FP8MAX * FP8MAX), None, AluOpType.mult))
        vs_consts1 = VS.n

        vg4 = vg.rearrange("p (t h v) -> p t h v", h=4, v=VCOL)
        ts_group_end = 0
        P.wait("sync", SS, ss_xn_done)

        for g in range(c.NG):
            if g == 0:
                for i2 in range(4):
                    P.wait("sync", WCS[i2], WCS[i2].n)
            g_wvb.wait(P, "sync")
            P.dma("sync", WVL, wvb[:, :], wv16_s[g, :, :])
            P.wait("vector", TS, ts_group_end)
            P.op("vector", VS, lambda e: e.memset(vg[:, :], 1.0))
            vs_vg_ones = VS.n


            for qk, (dst, src16) in enumerate(((qg, wq16_s), (kg, wk16_s))):
                for ei in range(4):
                    ec = 4 * g + ei
                    par = ei % 2
                    g_wb[par].wait(P, "sync")
                    P.dma("sync", QKL[par], wb[par][:, 0:SLABW], src16[ec, :, :])
                    P.wait("tensor", QKL[par])
                    banks = [pick_bank() for _ in range(c.NTW)]
                    for tt in range(c.NTW):
                        g_psf[banks[tt]].wait(P, "tensor")
                    for dc in range(c.ND):
                        for tt in range(c.NTW):
                            last = (dc == c.ND - 1 and tt == c.NTW - 1)
                            fn = lambda e, par=par, dc=dc, tt=tt, bk=banks[tt]: e.matmul(
                                PSF[bk][:, :], wb[par][:, 128 * dc:128 * (dc + 1)],
                                xn[dc][:, c.W * tt:c.W * (tt + 1)],
                                start=(dc == 0), stop=(dc == c.ND - 1))
                            if last:
                                P.op("tensor", TS, fn)
                            else:
                                P.opq("tensor", fn)
                    g_wb[par].set((TS, TS.n))
                    P.wait("scalar", TS)
                    for tt in range(c.NTW):
                        P.op("scalar", SS, lambda e, dst=dst, ei=ei, tt=tt, bk=banks[tt]: e.copy(
                            dst[ei][:, c.W * tt:c.W * (tt + 1)], PSF[bk][:, :]))
                        g_psf[banks[tt]].set((SS, SS.n))


            P.wait("tensor", VS, vs_vg_ones)
            P.wait("tensor", WVL, WVL.n)
            for tc in range(c.NT):
                bk = pick_bank()
                g_psf[bk].wait(P, "tensor")
                for dc in range(c.ND):
                    last = (dc == c.ND - 1)
                    fn = lambda e, bk=bk, tc=tc, dc=dc: e.matmul(
                        PSF[bk][:, :], xn[dc][:, 128 * tc:128 * (tc + 1)],
                        wvb[:, 512 * dc:512 * (dc + 1)],
                        start=(dc == 0), stop=(dc == c.ND - 1))
                    if last:
                        P.op("tensor", TS, fn)
                    else:
                        P.opq("tensor", fn)
                P.wait("vector", TS)
                P.op("vector", VS, lambda e, bk=bk, tc=tc: e.tensor_copy(
                    vg4[:, tc, :, 0:128],
                    PSF[bk][:, :].rearrange("p (h x) -> p h x", h=4)))
                g_psf[bk].set((VS, VS.n))
                if tc == c.NT - 1:
                    g_wvb.set((TS, TS.n))
            if DEBUG and g == 0:
                P.wait("gpsimd", VS)
                P.wait("gpsimd", SS)
                P.dma("gpsimd", GD, dbg_q[:, :], qg[0][:, :])
                P.dma("gpsimd", GD, dbg_k[:, :], kg[0][:, :])
                P.dma("gpsimd", GD, dbg_v[:, :], vg[:, 0:528])

            for hi in range(4):
                h = 4 * g + hi

                nqh = c.NJ // c.H
                for q in range(nqh):
                    jb = h * nqh + q
                    par2 = jb % 2
                    g_wf[par2].wait(P, "sync")
                    P.dma("sync", WF[par2], wf[par2][:, 0:SLABW], wfc_h[jb, :, :])
                    P.wait("scalar", WF[par2])
                    if jb == 0:
                        P.wait("scalar", VS, vs_consts1)
                    g_a8t[jb % 2].wait(P, "scalar")
                    P.op("scalar", SS, lambda e, par2=par2, jb=jb: e.activation(
                        a8t[jb % 2][:, 0:SLABW], wf[par2][:, 0:SLABW], AF.Copy,
                        scale=scb[:, 8:9]))
                    g_wf[par2].set((SS, SS.n))
                    P.wait("gpsimd", SS)
                    P.dma("gpsimd", A8S[jb % 2], wfc8_s[jb, :, :],
                          a8t[jb % 2][:, 0:SLABW])
                    g_a8t[jb % 2].set((A8S[jb % 2], A8S[jb % 2].n))

                for j in range(c.NT):
                    ntk = j // 4 + 1
                    for T in range(ntk):
                        bk = pick_bank()
                        diag = (T == j // 4)
                        g_psf[bk].wait(P, "tensor")
                        if diag:
                            P.opq("tensor", lambda e, hi=hi, j=j, T=T, bk=bk: e.matmul(
                                PSF[bk][:, :], qg[hi][:, 128 * j:128 * (j + 1)],
                                kg[hi][:, c.W * T:c.W * (T + 1)],
                                start=True, stop=False))
                            jm = j % 4
                            P.op("tensor", TS, lambda e, bk=bk, jm=jm: e.matmul(
                                PSF[bk][:, :], idb[:, :],
                                mk1[:, 384 - 128 * jm: 384 - 128 * jm + 512],
                                start=False, stop=True))
                        else:
                            P.op("tensor", TS, lambda e, hi=hi, j=j, T=T, bk=bk: e.matmul(
                                PSF[bk][:, :], qg[hi][:, 128 * j:128 * (j + 1)],
                                kg[hi][:, c.W * T:c.W * (T + 1)],
                                start=True, stop=True))
                        P.wait("vector", TS)
                        P.op("vector", VS, lambda e, bk=bk, T=T: e.reduce_max(
                            m_sb[:, T:T + 1], PSF[bk][:, :], axis=AX.X))
                        g_psf[bk].set((VS, VS.n))
                    P.sw("vector", VS)
                    P.op("vector", VS, lambda e, ntk=ntk, j=j: e.reduce_max(
                        mnegs[:, j:j + 1], m_sb[:, 0:ntk], axis=AX.X, negate=True))
                P.sw("vector", VS)
                g_mnegb.wait(P, "vector")
                P.op("vector", VS, lambda e: e.tensor_copy(
                    mnegb16[:, 0:c.NT], mnegs[:, 0:c.NT]))
                P.wait("tensor", VS)
                for j in range(c.NT):
                    tp = j % 2
                    g_pst[tp].wait(P, "tensor")
                    P.op("tensor", TS, lambda e, tp=tp, j=j: e.transpose(
                        ps_t[tp][0:1, 0:128], mnegb16[:, j:j + 1], idb[:, :]))
                    if j == c.NT - 1:
                        g_mnegb.set((TS, TS.n))
                    P.wait("scalar", TS)
                    P.op("scalar", SS, lambda e, j=j, tp=tp: e.copy(
                        mrow[0:1, 128 * j:128 * (j + 1)], ps_t[tp][0:1, 0:128]))
                    g_pst[tp].set((SS, SS.n))
                ss_mrow = SS.n
                if DEBUG and h == 0:
                    P.wait("gpsimd", SS, ss_mrow)
                    P.dma("gpsimd", GD, dbg_mrow[:, :], mrow[0:1, :])

                P.wait("tensor", SS, ss_mrow)
                for J in range(c.NTW):
                    nblk = 4 * J + 4
                    ss_exp = {}

                    def emit_sT(i):
                        par = i % 2
                        pbt = PSF[4 + par]
                        d = i - 4 * J
                        g_psf[4 + par].wait(P, "tensor")
                        P.opq("tensor", lambda e, hi=hi, i=i, J=J, pbt=pbt: e.matmul(
                            pbt[:, :], kg[hi][:, 128 * i:128 * (i + 1)],
                            qg[hi][:, c.W * J:c.W * (J + 1)],
                            start=True, stop=False))
                        if d >= 0:
                            P.opq("tensor", lambda e, J=J, pbt=pbt: e.matmul(
                                pbt[:, :], ones1[:, :],
                                mrow[0:1, c.W * J:c.W * (J + 1)],
                                start=False, stop=False))
                            P.op("tensor", TS, lambda e, pbt=pbt, d=d: e.matmul(
                                pbt[:, :], idb[:, :],
                                mk2[:, 384 - 128 * d: 384 - 128 * d + 512],
                                start=False, stop=True))
                        else:
                            P.op("tensor", TS, lambda e, J=J, pbt=pbt: e.matmul(
                                pbt[:, :], ones1[:, :],
                                mrow[0:1, c.W * J:c.W * (J + 1)],
                                start=False, stop=True))
                        ts_here = TS.n
                        P.wait("scalar", TS, ts_here)
                        g_E[par].wait(P, "scalar")
                        P.op("scalar", SS, lambda e, par=par, pbt=pbt: e.activation(
                            Ebuf[par][:, :], pbt[:, :], AF.Exp))
                        g_psf[4 + par].set((SS, SS.n))
                        ss_exp[i] = SS.n

                    def emit_ctx(i):
                        par = i % 2
                        P.wait("tensor", SS, ss_exp[i])
                        for q4 in range(4):
                            if i == 0:
                                g_psf[q4].wait(P, "tensor")
                            last = (i == nblk - 1 and q4 == 3)
                            fn = lambda e, par=par, q4=q4, i=i, hi=hi, nblk=nblk: e.matmul(
                                ps_w[q4][:, 0:129],
                                Ebuf[par][:, 128 * q4:128 * (q4 + 1)],
                                vg4[:, i, hi, 0:129],
                                start=(i == 0), stop=(i == nblk - 1))
                            if last:
                                P.op("tensor", TS, fn)
                            else:
                                P.opq("tensor", fn)
                        g_E[par].set((TS, TS.n))

                    for i in range(nblk):
                        emit_sT(i)
                        if i >= 1:
                            emit_ctx(i - 1)
                    emit_ctx(nblk - 1)
                    P.wait("vector", TS)
                    for q4 in range(4):
                        sc6 = 6 if q4 % 2 == 0 else 7
                        P.op("vector", VS, lambda e, q4=q4, sc6=sc6: e.reciprocal(
                            stat[:, sc6:sc6 + 1], ps_w[q4][:, 128:129]))
                        P.sw("vector", VS)
                        P.op("vector", VS, lambda e, q4=q4, J=J, sc6=sc6: e.tensor_scalar(
                            ctv_head[:, 128 * (4 * J + q4):128 * (4 * J + q4 + 1)],
                            ps_w[q4][:, 0:128], stat[:, sc6:sc6 + 1],
                            None, AluOpType.mult))
                        g_psf[q4].set((VS, VS.n))
                P.wait("tensor", VS)
                g_ctv[0].wait(P, "vector") if False else None
                for jj in range(c.NT):
                    cp = jj % 2
                    g_pst[cp].wait(P, "tensor")
                    P.op("tensor", TS, lambda e, cp=cp, jj=jj: e.transpose(
                        ps_t[cp][:, 0:128],
                        ctv_head[:, 128 * jj:128 * (jj + 1)], idb[:, :]))
                    P.wait("scalar", TS)
                    g_ctv2[cp].wait(P, "scalar")
                    P.op("scalar", SS, lambda e, cp=cp: e.copy(
                        ctv2[cp][:, :], ps_t[cp][:, 0:128]))
                    g_pst[cp].set((SS, SS.n))
                    P.wait("gpsimd", SS)
                    P.dma("gpsimd", CTV2S[cp],
                          ctx_s[h, :, 128 * jj:128 * (jj + 1)],
                          ctv2[cp][:, :])
                    g_ctv2[cp].set((CTV2S[cp], CTV2S[cp].n))
            ts_group_end = TS.n
        ts_attn_end = TS.n
        a8s_p2 = [s.n for s in A8S]


        P.wait("scalar", VS, vs_consts1)
        P.wait("scalar", TS, ts_attn_end)
        for dc in range(c.ND):
            P.op("scalar", SS, lambda e, dc=dc: e.activation(
                x8[dc][:, :], xn[dc][:, :], AF.Copy, scale=scb[:, 4:5]))
        ss_x8 = SS.n

        P.wait("sync", SS, ss_x8)
        WOL = Sem(es, nc, "WOL")

        for i2 in range(2):
            P.wait("sync", A8S[i2], a8s_p2[i2])
        P.wait("tensor", SS, ss_x8)
        for jb in range(c.NJ):
            if jb % 4 == 0 and jb // 4 < c.ND:
                dc2 = jb // 4
                P.dma("sync", WOL, wob[dc2][:, :],
                      wo16_s[128 * dc2:128 * (dc2 + 1), :])
            par = jb % 2
            g_w8b[par].wait(P, "sync")
            P.dma("sync", W8L[par], w8b[par][:, 0:SLABW], wfc8_s[jb, :, :])
            P.wait("tensor", W8L[par])
            banks = [pick_bank() for _ in range(c.NTW)]
            for tt in range(c.NTW):
                g_psf[banks[tt]].wait(P, "tensor")
            NDP = c.ND // 2
            for dp in range(NDP):
                wpair = w8b[par][:, 256 * dp:256 * (dp + 1)].rearrange(
                    "p (i m) -> p i m", i=2)
                xpair_full = Bar8[:, 2 * dp * c.S:(2 * dp + 2) * c.S].rearrange(
                    "p (i t) -> p i t", i=2)
                for tt in range(c.NTW):
                    last = (dp == NDP - 1 and tt == c.NTW - 1)
                    fn = lambda e, wp=wpair, xp=xpair_full, dp=dp, tt=tt, bk=banks[tt]: e.matmul(
                        PSF[bk][:, :], wp,
                        xp[:, :, c.W * tt:c.W * (tt + 1)],
                        start=(dp == 0), stop=(dp == NDP - 1),
                        perf_mode=MPM.DoubleRow)
                    if last:
                        P.op("tensor", TS, fn)
                    else:
                        P.opq("tensor", fn)
            g_w8b[par].set((TS, TS.n))
            P.wait("scalar", TS)
            for tt in range(c.NTW):
                g_evb[tt].wait(P, "scalar")
                P.op("scalar", SS, lambda e, tt=tt, bk=banks[tt]: e.activation(
                    evb[tt][:, :], PSF[bk][:, :], AF.Relu, scale=scb[:, 5:6]))
                g_psf[banks[tt]].set((SS, SS.n))
                g_abuf[tt].wait(P, "scalar")
                P.sw("scalar", SS)
                P.op("scalar", SS, lambda e, tt=tt: e.activation(
                    abuf[tt][:, :], evb[tt][:, :], AF.Square))
                ss_sq = SS.n
                P.wait("vector", SS, ss_sq)
                P.op("vector", VS, lambda e, tt=tt: e.reduce_max(
                    stat[:, 8:9], evb[tt][:, :], axis=AX.X))
                P.sw("vector", VS)
                P.op("vector", VS, lambda e: e.tensor_tensor(
                    hmax[:, :], hmax[:, :], stat[:, 8:9], AluOpType.max))
                g_evb[tt].set((VS, VS.n))
                P.wait("gpsimd", SS, ss_sq)
                P.dma("gpsimd", ABS[tt], a_s[128 * jb:128 * (jb + 1),
                                             c.W * tt:c.W * (tt + 1)], abuf[tt][:, :])
                g_abuf[tt].set((ABS[tt], ABS[tt].n))
            if jb < c.n_wpq:
                tpq = c.n_wpq // c.NDW
                ee, off = jb // tpq, (jb % tpq) * c.WSL
                g_wf[par].wait(P, "sync")
                P.dma("sync", WF[par], wf[par][:, 0:c.WSL], wp_h[ee, :, off:off + c.WSL])
                P.wait("scalar", WF[par])
                g_a8t[jb % 2].wait(P, "scalar")
                P.op("scalar", SS, lambda e, par=par, jb=jb: e.activation(
                    a8t[jb % 2][:, 0:c.WSL], wf[par][:, 0:c.WSL], AF.Copy,
                    scale=scb[:, 9:10]))
                g_wf[par].set((SS, SS.n))
                P.wait("gpsimd", SS)
                P.dma("gpsimd", A8S[jb % 2], wp8_s[ee, :, off:off + c.WSL],
                      a8t[jb % 2][:, 0:c.WSL])
                g_a8t[jb % 2].set((A8S[jb % 2], A8S[jb % 2].n))
        vs_fc_end = VS.n
        abs_done = [s.n for s in ABS]

        P.wait("scalar", VS, vs_fc_end)
        P.op("scalar", SS, lambda e: e.activation(
            stat[:, 15:16], hmax[:, :], AF.Square))
        P.wait("vector", SS)
        P.op("vector", VS, lambda e: e.tensor_copy(mcols[:, 0:1], stat[:, 15:16]))
        P.wait("tensor", VS)
        g_psf[4].wait(P, "tensor")
        P.op("tensor", TS, lambda e: e.transpose(
            ps_s[0][0:4, 0:128], mcols[:, 0:4], idf[:, :]))
        P.wait("vector", TS)
        P.op("vector", VS, lambda e: e.reduce_max(
            tmp4[0:4, 0:1], ps_s[0][0:4, 0:128], axis=AX.X))
        g_psf[4].set((VS, VS.n))
        vs_cc2 = VS.n
        P.wait("gpsimd", VS, vs_cc2)
        if DEBUG:
            P.dma("gpsimd", GD, dbg_hm[:, 0:1], hmax[:, :], slow=True)
            P.dma("gpsimd", GD, dbg_hm[:, 1:2], stat[:, 15:16], slow=True)
        P.dma("gpsimd", BCS, cc2_i[:, :], zrow[:, :])
        P.wait("gpsimd", BCS)
        P.dma("gpsimd", BCS, cc2_i[0:4, 0:1], tmp4[0:4, 0:1], slow=True)
        P.wait("gpsimd", BCS)
        P.op("gpsimd", CC, lambda e: e.collective_compute(
            "AllReduce", AluOpType.max,
            replica_groups=[list(range(c.n_cores))],
            ins=[cc2_i.opt()], outs=[cc2_o.opt()]))

        def emit_p6(jc):
            par = jc % 2
            if jc == 0:
                P.wait("scalar", VS, vs_consts2)
                for tt in range(4):
                    P.wait("sync", ABS[tt], abs_done[tt])
            g_abf[par].wait(P, "sync")
            P.dma("sync", ABFS[par], abf[par][:, 0:c.S], a_s[128 * jc:128 * (jc + 1), :])
            P.wait("scalar", ABFS[par])
            g_a8t[par].wait(P, "scalar")
            P.op("scalar", SS, lambda e, par=par: e.activation(
                a8t[par][:, 0:c.S], abf[par][:, 0:c.S], AF.Copy,
                scale=scb[:, 6:7]))
            g_abf[par].set((SS, SS.n))
            P.wait("gpsimd", SS)
            for T in range(c.NTW):
                P.dma("gpsimd", A8S[par], a8_s[T, :, c.W * jc:c.W * (jc + 1)],
                      a8t[par][:, c.W * T:c.W * (T + 1)])
            g_a8t[par].set((A8S[par], A8S[par].n))

        P.wait("sync", VS, vs_fc_end)
        P.wait("tensor", WOL, WOL.n)
        r1s_pass0 = [0, 0]
        ts_wo_pass = 0
        vs_consts2 = None
        for p5 in range(2):
            if p5 == 1:
                P.wait("gpsimd", CC)
                P.dma("gpsimd", BCS, scb[:, 3:4], cc2_o[0:1, 0:1].partition_broadcast(128))
                P.wait("vector", BCS, BCS.n)
                P.op("vector", VS, lambda e: e.tensor_scalar(
                    scb[:, 10:11], scb[:, 3:4], EPS_CLAMP, None, AluOpType.max))
                P.sw("vector", VS)
                P.op("vector", VS, lambda e: e.reciprocal(scb[:, 11:12], scb[:, 10:11]))
                P.sw("vector", VS)
                P.op("vector", VS, lambda e: e.tensor_scalar(
                    scb[:, 6:7], scb[:, 11:12], FP8MAX / 2.0, None, AluOpType.mult))
                P.op("vector", VS, lambda e: e.tensor_scalar(
                    scb[:, 11:12], scb[:, 2:3], EPS_CLAMP, None, AluOpType.max))
                P.sw("vector", VS)
                P.op("vector", VS, lambda e: e.tensor_tensor(
                    scb[:, 7:8], scb[:, 10:11], scb[:, 11:12], AluOpType.mult))
                P.sw("vector", VS)
                P.op("vector", VS, lambda e: e.tensor_scalar(
                    scb[:, 7:8], scb[:, 7:8], 4.0 / (FP8MAX * FP8MAX), None, AluOpType.mult))
                vs_consts2 = VS.n
            for k in range(NH):
                dc = NH * p5 + k
                if p5 == 0 and k == 0:
                    P.wait("gpsimd", TS, ts_attn_end)
                    for cp2 in range(2):
                        P.wait("gpsimd", CTV2S[cp2], CTV2S[cp2].n)
                if p5 == 1 and k == 0:
                    P.wait("gpsimd", TS, ts_wo_pass)
                P.dma("gpsimd", CTXL, ctxh[k][:, :], ctx_s[dc, :, :])
            ctxl_ld = CTXL.n
            if DEBUG and p5 == 0:
                P.wait("gpsimd", CTXL, ctxl_ld)
                P.dma("gpsimd", GD, dbg_ctx[:, :], ctxh[0][:, :])
            P.wait("tensor", CTXL, ctxl_ld)
            if p5 == 1:
                P.wait("sync", R1S[0], r1s_pass0[0])
                P.wait("sync", R1S[1], r1s_pass0[1])
            for tc in range(c.NT):
                if p5 == 1:
                    for jc6 in range(tc * 4, tc * 4 + 4):
                        if jc6 < c.NJ:
                            emit_p6(jc6)
                banks = [pick_bank() for _ in range(c.NDW)]
                for ee in range(c.NDW):
                    g_psf[banks[ee]].wait(P, "tensor")
                for k in range(NH):
                    dc = NH * p5 + k
                    for ee in range(c.NDW):
                        last = (k == NH - 1 and ee == c.NDW - 1)
                        fn = lambda e, k=k, dc=dc, tc=tc, ee=ee, bk=banks[ee]: e.matmul(
                            PSF[bk][:, :], ctxh[k][:, 128 * tc:128 * (tc + 1)],
                            wob[dc][:, c.W * ee:c.W * (ee + 1)],
                            start=(k == 0), stop=(k == NH - 1))
                        if last:
                            P.op("tensor", TS, fn)
                        else:
                            P.opq("tensor", fn)
                ts_wo = TS.n
                for ee in range(c.NDW):
                    par = (tc * c.NDW + ee) % 2
                    g_xs[par].wait(P, "sync")
                    src = x_h if p5 == 0 else r1_s
                    P.dma("sync", XSS[par], xs[par][:, :],
                          src[128 * tc:128 * (tc + 1), c.W * ee:c.W * (ee + 1)])
                    P.wait("vector", XSS[par])
                    P.wait("vector", TS, ts_wo)
                    g_r1t[par].wait(P, "vector")
                    P.op("vector", VS, lambda e, par=par, bk=banks[ee]: e.tensor_tensor(
                        r1t[par][:, :], PSF[bk][:, :], xs[par][:, :],
                        AluOpType.add))
                    g_xs[par].set((VS, VS.n))
                    g_psf[banks[ee]].set((VS, VS.n))
                    P.wait("gpsimd", VS)
                    P.dma("gpsimd", R1S[par], r1_s[128 * tc:128 * (tc + 1),
                                                   c.W * ee:c.W * (ee + 1)],
                          r1t[par][:, :])
                    g_r1t[par].set((R1S[par], R1S[par].n))
            if p5 == 0:
                r1s_pass0 = [s.n for s in R1S]
                ts_wo_pass = TS.n


        ss_a8_end = SS.n
        ts_prev_slab = 0
        for i2 in range(2):
            P.wait("sync", A8S[i2], A8S[i2].n)
            P.wait("sync", R1S[i2], R1S[i2].n)
            P.wait("gpsimd", A8S[i2], A8S[i2].n)
        slab_i = 0
        ts_slab_hist = {}
        for ee in range(c.NDW):
            P.wait("sync", TS, ts_prev_slab)
            if ee == 0:
                P.wait("sync", SS, ss_a8_end)
            P.dma("sync", WP8L, wp8sl[:, :], wp8_s[ee, :, :])
            for T in range(c.NTW):
                sp = slab_i % 2
                P.wait("gpsimd", TS, ts_slab_hist.get(slab_i - 2, 0))
                P.dma("gpsimd", A8L, a8sl[sp][:, :], a8_s[T, :, :])
                a8l_here = A8L.n
                P.wait("tensor", A8L, a8l_here)
                P.wait("tensor", WP8L, WP8L.n)
                a8v = a8sl[sp].rearrange("p (j t) -> p j t", t=c.W)
                wpv = wp8sl.rearrange("p (j t) -> p j t", t=c.W)
                NJP = c.NJ // 2
                for tc in range(4):
                    bk = pick_bank()
                    g_psf[bk].wait(P, "tensor")
                    for jp in range(NJP):
                        last = (jp == NJP - 1)
                        fn = lambda e, bk=bk, jp=jp, tc=tc, a8v=a8v, wpv=wpv: e.matmul(
                            PSF[bk][:, :],
                            a8v[:, 2 * jp:2 * jp + 2, 128 * tc:128 * (tc + 1)],
                            wpv[:, 2 * jp:2 * jp + 2, :],
                            start=(jp == 0), stop=(jp == NJP - 1),
                            perf_mode=MPM.DoubleRow)
                        if last:
                            P.op("tensor", TS, fn)
                        else:
                            P.opq("tensor", fn)
                    ts_mm = TS.n
                    if DEBUG and ee == 0 and T == 0 and tc == 0:
                        P.wait("vector", TS, ts_mm)
                        P.op("vector", VS, lambda e: e.tensor_copy(
                            wf[0][:, 0:512], ps_s[0][:, :]))
                        P.wait("gpsimd", VS)
                        P.dma("gpsimd", GD, dbg_ps[:, :], wf[0][:, 0:512])
                        P.dma("gpsimd", GD, dbg_wpsl[:, :], wp8sl[:, :])
                    row = 128 * (4 * T + tc)
                    par = (T * 4 + tc) % 2
                    g_xs[par].wait(P, "sync")
                    P.dma("sync", XSS[par], xs[par][:, :],
                          r1_s[row:row + 128, c.W * ee:c.W * (ee + 1)])
                    P.wait("vector", XSS[par])
                    if DEBUG and ee == 0 and T == 0 and tc == 0:
                        P.wait("gpsimd", XSS[par], XSS[par].n)
                        P.dma("gpsimd", GD, dbg_xs[:, :], xs[par][:, :])
                    P.wait("vector", TS, ts_mm)
                    g_outt[par].wait(P, "vector")
                    P.op("vector", VS, lambda e, bk=bk, par=par: e.tensor_scalar(
                        outt[par][:, :], PSF[bk][:, :], scb[:, 7:8], None,
                        AluOpType.mult))
                    P.sw("vector", VS)
                    P.op("vector", VS, lambda e, par=par: e.tensor_tensor(
                        outt[par][:, :], outt[par][:, :], xs[par][:, :],
                        AluOpType.add))
                    g_psf[bk].set((VS, VS.n))
                    g_xs[par].set((VS, VS.n))
                    P.wait("gpsimd", VS)
                    P.dma("gpsimd", OUTS[par],
                          out_h[row:row + 128, c.W * ee:c.W * (ee + 1)],
                          outt[par][:, :])
                    g_outt[par].set((OUTS[par], OUTS[par].n))
                ts_slab_hist[slab_i] = TS.n
                slab_i += 1
                ts_prev_slab = TS.n

        if DEBUG:
            P.wait("gpsimd", VS, vs_consts2)
            P.dma("gpsimd", GD, dbg_scb2[:, :], scb[:, :])
            P.wait("gpsimd", ST, ST.n)
            P.dma("gpsimd", GD, dbg_r1[:, :], r1_s[:, :])
            P.dma("gpsimd", GD, dbg_as[:, :], a_s[:, :])
            P.dma("gpsimd", GD, dbg_cs[:, :, :], ctx_s[:, :, :])
            P.dma("gpsimd", GD, dbg_wp8[:, :, :], wp8_s[:, :, :])
            P.dma("gpsimd", GD, dbg_a8[:, :, :], a8_s[:, :, :])
            P.wait("gpsimd", GD, GD.n)
        P.wait("gpsimd", OUTS[0], OUTS[0].n)
        P.wait("gpsimd", OUTS[1], OUTS[1].n)

        @blk.sync
        def _(e):
            for f in P.ops["sync"]:
                f(e)

        @blk.vector
        def _(e):
            for f in P.ops["vector"]:
                f(e)

        @blk.tensor
        def _(e):
            for f in P.ops["tensor"]:
                f(e)

        @blk.scalar
        def _(e):
            for f in P.ops["scalar"]:
                f(e)

        @blk.gpsimd
        def _(e):
            for f in P.ops["gpsimd"]:
                f(e)

    return nc


def _host_blocked(c: Cfg, Wq, Wk, Wv, Wo, w_fc, w_proj):
    f32 = np.float32
    wqT = np.ascontiguousarray(Wq.astype(f32, copy=False).T)
    wkT = np.ascontiguousarray(Wk.astype(f32, copy=False).T)
    wvT = np.ascontiguousarray(Wv.astype(f32, copy=False).T)
    woT = np.ascontiguousarray(Wo.astype(f32, copy=False).T)
    wfcT = np.ascontiguousarray(w_fc.astype(f32, copy=False).T)
    wpT = np.ascontiguousarray(w_proj.astype(f32, copy=False).T)

    def blk_lhsT(WT, PM):
        K, M = WT.shape
        nkc, nmc = K // 128, M // PM
        a = WT.reshape(nkc, 128, nmc, PM).transpose(2, 1, 0, 3)
        return np.ascontiguousarray(a).reshape(nmc, 128, nkc * PM)

    wqTb = blk_lhsT(wqT, 128)
    wkTb = blk_lhsT(wkT, 128)
    wfcTb = blk_lhsT(wfcT, 128)
    a = wvT.reshape(c.ND, 128, c.NG, 512).transpose(2, 1, 0, 3)
    wvTb = np.ascontiguousarray(a).reshape(c.NG, 128, c.ND * 512)
    a = wpT.reshape(c.NJ, 128, c.NDW, c.W).transpose(2, 1, 0, 3)
    wpTb = np.ascontiguousarray(a).reshape(c.NDW, 128, c.NJ * c.W)
    return wqTb, wkTb, wvTb, woT, wfcTb, wpTb


def _host_masks():
    bf = ml_dtypes.bfloat16
    m1 = np.zeros((128, 1024), dtype=np.float32)
    tri_u = np.where(np.arange(128)[None, :] > np.arange(128)[:, None], BIGNEG, 0.0)
    m1[:, 384:512] = tri_u
    m1[:, 512:] = BIGNEG
    m2 = np.zeros((128, 1024), dtype=np.float32)
    tri_l = np.where(np.arange(128)[:, None] > np.arange(128)[None, :], BIGNEG, 0.0)
    m2[:, 0:384] = BIGNEG
    m2[:, 384:512] = tri_l
    return m1.astype(bf), m2.astype(bf)


def make_inputs(c: Cfg, x_full, Wq, Wk, Wv, Wo, w_fc, w_proj):
    bf = ml_dtypes.bfloat16
    wqTb, wkTb, wvTb, woT, wfcTb, wpTb = _host_blocked(c, Wq, Wk, Wv, Wo, w_fc, w_proj)
    m1, m2 = _host_masks()
    identb = np.eye(128, dtype=bf)
    identf = np.eye(128, dtype=np.float32)
    onesb = np.ones((1, 128), dtype=bf)

    fc_flat = np.ascontiguousarray(wfcTb.transpose(1, 0, 2)).reshape(128, -1)
    wp_flat = np.ascontiguousarray(wpTb.transpose(1, 0, 2)).reshape(128, -1)
    n1, n2 = c.n_wsl_fc, c.n_wsl_wp
    in_maps = []
    for core in range(c.n_cores):
        s1 = fc_flat[:, core * n1 * c.WSL:(core + 1) * n1 * c.WSL]
        s1 = np.ascontiguousarray(s1.reshape(128, n1, c.WSL).transpose(1, 0, 2))
        s2 = wp_flat[:, core * n2 * c.WSL:(core + 1) * n2 * c.WSL]
        s2 = np.ascontiguousarray(s2.reshape(128, n2, c.WSL).transpose(1, 0, 2))
        wslice = np.ascontiguousarray(np.concatenate([s1, s2], axis=0))
        in_maps.append({
            "x": np.ascontiguousarray(x_full[core]),
            "wqTb": wqTb, "wkTb": wkTb, "wvTb": wvTb, "woT": woT,
            "wfcTb": wfcTb, "wpTb": wpTb, "wslice": wslice,
            "identb": identb, "identf": identf, "onesb": onesb,
            "mask1": m1, "mask2": m2,
        })
    return in_maps


def kernel(L0_attn_x, L0_attn_Wq, L0_attn_Wk, L0_attn_Wv, L0_attn_Wo,
           L0_ffn_w_fc, L0_ffn_w_proj):
    c = Cfg(S=2048, D=2048, H=16, n_cores=8)
    nc = build_nc(c)
    in_maps = make_inputs(c, np.asarray(L0_attn_x),
                          np.asarray(L0_attn_Wq), np.asarray(L0_attn_Wk),
                          np.asarray(L0_attn_Wv), np.asarray(L0_attn_Wo),
                          np.asarray(L0_ffn_w_fc), np.asarray(L0_ffn_w_proj))
    res = run_bass_kernel_spmd(nc, in_maps, core_ids=list(range(c.n_cores)))
    out = np.stack([np.asarray(r["out"]) for r in res.results], axis=0)
    return out.astype(np.float32, copy=False)


# version 15
# speedup vs baseline: 1.0231x; 1.0231x over previous
import contextlib
import numpy as np
import ml_dtypes

import concourse.bass as bass
import concourse.mybir as mybir
from concourse.bass_utils import run_bass_kernel_spmd
from concourse.alu_op_type import AluOpType

F32 = mybir.dt.float32
BF16 = mybir.dt.bfloat16
F8 = mybir.dt.float8e4
AF = mybir.ActivationFunctionType
AX = mybir.AxisListType
MPM = mybir.MatmulPerfMode

BIGNEG = -1.0e30
FP8MAX = 448.0
EPS_CLAMP = FP8MAX * 1e-6
DEBUG = False


class Cfg:
    def __init__(self, S=2048, D=2048, H=16, n_cores=8):
        self.S, self.D, self.H = S, D, H
        self.DH = 128
        assert D == H * self.DH
        self.J = 4 * D
        self.NT = S // 128
        self.ND = D // 128
        self.NJ = self.J // 128
        self.W = 512
        self.NTW = S // self.W
        self.NDW = D // self.W
        self.NG = H // 4
        self.n_cores = n_cores
        self.SCALE = float(self.DH) ** 0.5
        self.WSL = 2048
        ntile_fc = (self.J * self.D) // (128 * self.WSL)
        assert ntile_fc % n_cores == 0
        self.n_wsl_fc = ntile_fc // n_cores
        self.n_wsl_wp = ntile_fc // n_cores
        self.n_wsl = self.n_wsl_fc + self.n_wsl_wp
        self.n_wpq = (self.NDW * self.NJ * self.W) // self.WSL


class Sem:
    def __init__(self, es, nc, name):
        self.h = es.enter_context(nc.semaphore(name))
        self.n = 0


class Guard:
    def __init__(self):
        self.deps = []

    def set(self, *deps):
        self.deps = list(deps)

    def wait(self, P, eng):
        for sem, val in self.deps:
            P.wait(eng, sem, val)


class B:
    def __init__(self):
        self.ops = {e: [] for e in ("sync", "vector", "tensor", "scalar", "gpsimd")}

    def emit(self, eng, fn):
        self.ops[eng].append(fn)

    def wait(self, eng, sem, val=None):
        v = sem.n if val is None else val
        if v > 0:
            self.ops[eng].append(lambda e, s=sem, vv=v: e.wait_ge(s.h, vv))

    def dma(self, eng, sem, out, in_, slow=False):
        def f(e, out=out, in_=in_, slow=slow):
            e.dma_start(out=out, in_=in_,
                        allow_slow_non_contiguous=slow).then_inc(sem.h, 16)
        self.emit(eng, f)
        sem.n += 16

    def op(self, eng, sem, fn, amt=1):
        def f(e, fn=fn):
            fn(e).then_inc(sem.h, amt)
        self.emit(eng, f)
        sem.n += amt

    def opq(self, eng, fn):
        self.emit(eng, fn)

    def sw(self, eng, sem):
        self.wait(eng, sem, sem.n)


def build_nc(c: Cfg):
    nc = bass.Bass("TRN2", target_bir_lowering=False, debug=False,
                   detect_race_conditions=False)
    SLABW = c.ND * 128

    x_h = nc.dram_tensor("x", [c.S, c.D], F32, kind="ExternalInput").ap()
    wq_h = nc.dram_tensor("wqTb", [c.ND, 128, SLABW], F32, kind="ExternalInput").ap()
    wk_h = nc.dram_tensor("wkTb", [c.ND, 128, SLABW], F32, kind="ExternalInput").ap()
    wv_h = nc.dram_tensor("wvTb", [c.NG, 128, c.ND * 512], F32, kind="ExternalInput").ap()
    wo_h = nc.dram_tensor("woT", [c.D, c.D], F32, kind="ExternalInput").ap()
    wfc_h = nc.dram_tensor("wfcTb", [c.NJ, 128, SLABW], F32, kind="ExternalInput").ap()
    wp_h = nc.dram_tensor("wpTb", [c.NDW, 128, c.NJ * c.W], F32, kind="ExternalInput").ap()
    wsl_h = nc.dram_tensor("wslice", [c.n_wsl, 128, c.WSL], F32, kind="ExternalInput").ap()
    idb_h = nc.dram_tensor("identb", [128, 128], BF16, kind="ExternalInput").ap()
    idf_h = nc.dram_tensor("identf", [128, 128], F32, kind="ExternalInput").ap()
    ones_h = nc.dram_tensor("onesb", [1, 128], BF16, kind="ExternalInput").ap()
    mk1_h = nc.dram_tensor("mask1", [128, 1024], BF16, kind="ExternalInput").ap()
    mk2_h = nc.dram_tensor("mask2", [128, 1024], BF16, kind="ExternalInput").ap()

    out_h = nc.dram_tensor("out", [c.S, c.D], F32, kind="ExternalOutput").ap()

    ctx_s = nc.dram_tensor("ctx_scr", [c.ND, 128, c.S], BF16).ap()
    a_s = nc.dram_tensor("a_scr", [c.J, c.S], BF16).ap()
    a8_s = nc.dram_tensor("a8_scr", [c.NTW, 128, c.NJ * c.W], F8).ap()
    wp8_s = nc.dram_tensor("wp8_scr", [c.NDW, 128, c.NJ * c.W], F8).ap()
    wfc8_s = nc.dram_tensor("wfc8_scr", [c.NJ, 128, c.ND * 128], F8).ap()
    wq16_s = nc.dram_tensor("wq16_scr", [c.ND, 128, c.ND * 128], BF16).ap()
    wk16_s = nc.dram_tensor("wk16_scr", [c.ND, 128, c.ND * 128], BF16).ap()
    wv16_s = nc.dram_tensor("wv16_scr", [c.NG, 128, c.ND * 512], BF16).ap()
    wo16_s = nc.dram_tensor("wo16_scr", [c.D, c.D], BF16).ap()
    r1_s = nc.dram_tensor("r1_scr", [c.S, c.D], F32).ap()
    cc1_i = nc.dram_tensor("cc1_in", [4, 8], F32).ap()
    cc1_o = nc.dram_tensor("cc1_out", [4, 8], F32).ap()
    cc2_i = nc.dram_tensor("cc2_in", [4, 8], F32).ap()
    cc2_o = nc.dram_tensor("cc2_out", [4, 8], F32).ap()
    if DEBUG:
        dbg_xn = nc.dram_tensor("dbg_xn", [128, c.S], BF16, kind="ExternalOutput").ap()
        dbg_t4 = nc.dram_tensor("dbg_t4", [128, 1], F32, kind="ExternalOutput").ap()
        dbg_scb1 = nc.dram_tensor("dbg_scb1", [128, 4], F32, kind="ExternalOutput").ap()
        dbg_q = nc.dram_tensor("dbg_q", [128, c.S], BF16, kind="ExternalOutput").ap()
        dbg_k = nc.dram_tensor("dbg_k", [128, c.S], BF16, kind="ExternalOutput").ap()
        dbg_v = nc.dram_tensor("dbg_v", [128, 528], BF16, kind="ExternalOutput").ap()
        dbg_mrow = nc.dram_tensor("dbg_mrow", [1, c.S], BF16, kind="ExternalOutput").ap()
        dbg_ctx = nc.dram_tensor("dbg_ctx", [128, c.S], BF16, kind="ExternalOutput").ap()
        dbg_a = nc.dram_tensor("dbg_a", [128, c.S], BF16, kind="ExternalOutput").ap()
        dbg_hm = nc.dram_tensor("dbg_hm", [128, 2], F32, kind="ExternalOutput").ap()
        dbg_scb2 = nc.dram_tensor("dbg_scb2", [128, 12], F32, kind="ExternalOutput").ap()
        dbg_r1 = nc.dram_tensor("dbg_r1", [c.S, c.D], F32, kind="ExternalOutput").ap()
        dbg_as = nc.dram_tensor("dbg_as", [c.J, c.S], BF16, kind="ExternalOutput").ap()
        dbg_cs = nc.dram_tensor("dbg_cs", [c.ND, 128, c.S], BF16, kind="ExternalOutput").ap()
        dbg_wp8 = nc.dram_tensor("dbg_wp8", [c.NDW, 128, c.NJ * c.W], F8, kind="ExternalOutput").ap()
        dbg_a8 = nc.dram_tensor("dbg_a8", [c.NTW, 128, c.NJ * c.W], F8, kind="ExternalOutput").ap()
        dbg_ps = nc.dram_tensor("dbg_ps", [128, 512], F32, kind="ExternalOutput").ap()
        dbg_wpsl = nc.dram_tensor("dbg_wpsl", [128, c.NJ * c.W], F8, kind="ExternalOutput").ap()
        dbg_xs = nc.dram_tensor("dbg_xs", [128, 512], F32, kind="ExternalOutput").ap()

    es = contextlib.ExitStack()
    with es:
        sbuf = lambda n, s, d: es.enter_context(nc.sbuf_tensor(n, s, d))
        psumt = lambda n, s, d=F32: es.enter_context(nc.psum_tensor(n, s, d))

        A = sbuf("arenaA", [128, 32 * 1024], BF16)
        xn = [A[:, i * c.S:(i + 1) * c.S] for i in range(c.ND)]
        wob = [A[:, i * c.D:(i + 1) * c.D] for i in range(c.ND)]

        Bar = sbuf("arenaB", [128, 32 * 1024], BF16)
        qg = [Bar[:, i * c.S:(i + 1) * c.S] for i in range(4)]
        kg = [Bar[:, (4 + i) * c.S:(5 + i) * c.S] for i in range(4)]
        VCOL = 132
        vg = Bar[:, 8 * c.S: 8 * c.S + c.NT * 4 * VCOL]
        EOFF = 8 * c.S + c.NT * 4 * VCOL
        Ebuf = [Bar[:, EOFF + i * c.W: EOFF + (i + 1) * c.W] for i in range(2)]
        Bar8 = Bar.bitcast(F8)
        x8 = [Bar8[:, i * c.S:(i + 1) * c.S] for i in range(c.ND)]
        NH = c.ND // 2
        ctxh = [Bar[:, (8 + i) * c.S:(9 + i) * c.S] for i in range(NH)]
        a8sl = [Bar8[:, 0: c.NJ * c.W], Bar8[:, 16 * c.S: 16 * c.S + c.NJ * c.W]]

        Car = sbuf("arenaC", [128, 6 * 1024], F32)
        wf = [Car[:, i * 2048:(i + 1) * 2048] for i in range(2)]
        xt2 = [Car[:, 4096: 4096 + 2048], None]
        xs = [Car[:, 4096 + i * c.W: 4096 + (i + 1) * c.W] for i in range(2)]
        r1t = [Car[:, 5120 + i * c.W: 5120 + (i + 1) * c.W] for i in range(2)]
        CarB = Car.bitcast(BF16)
        abf = [CarB[:, i * 2048: (i + 1) * 2048] for i in range(2)]

        Dar = sbuf("arenaD", [128, 20 * 1024], BF16)
        wb = [Dar[:, i * 2048:(i + 1) * 2048] for i in range(4)]
        wvb = Dar[:, 2 * 2048: 2 * 2048 + c.ND * 512]
        DarF = Dar.bitcast(F32)
        sqD = DarF[:, 6144: 6144 + 2048]
        xt2[1] = DarF[:, 4096: 4096 + 2048]
        evb = [Dar[:, 2 * 2048 + i * c.W: 2 * 2048 + (i + 1) * c.W] for i in range(4)]
        abuf = [Dar[:, 2 * 2048 + (4 + i) * c.W: 2 * 2048 + (5 + i) * c.W] for i in range(4)]
        Dar8 = Dar.bitcast(F8)
        w8b = [Dar8[:, 16 * 2048 + i * 2048: 16 * 2048 + (i + 1) * 2048] for i in range(2)]
        a8t = [Dar8[:, 18 * 2048 + i * 2048: 18 * 2048 + (i + 1) * 2048] for i in range(2)]
        outt = [DarF[:, 8192 + i * c.W: 8192 + (i + 1) * c.W] for i in range(2)]
        wp8sl = Dar8[:, 0: c.NJ * c.W]

        mk1 = sbuf("mk1", [128, 1024], BF16)
        mk2 = sbuf("mk2", [128, 1024], BF16)
        idb = sbuf("idb", [128, 128], BF16)
        idf = sbuf("idf", [128, 128], F32)
        ones1 = sbuf("ones1", [1, 128], BF16)
        stat = sbuf("stat", [128, 16], F32)
        xmax = sbuf("xmax", [128, 1], F32)
        wmax = sbuf("wmax", [128, 2], F32)
        hmax = sbuf("hmax", [128, 1], F32)
        mcols = sbuf("mcols", [128, 4], F32)
        tmp4 = sbuf("tmp4", [128, 1], F32)
        zrow = sbuf("zrow", [4, 8], F32)
        m_sb = sbuf("m_sb", [128, 4], F32)
        mneg = sbuf("mneg", [128, 1], F32)
        mnegb = sbuf("mnegb", [128, 1], BF16)
        mnegs2 = sbuf("mnegs2", [128, 16], F32)
        m_sb2 = sbuf("m_sb2", [128, 16], F32)
        rstd_sb = sbuf("rstd_sb", [128, 16], F32)
        mnegs = sbuf("mnegs", [128, 16], F32)
        mnegb16 = sbuf("mnegb16", [128, 16], BF16)
        mrow = CarB[0:1, 8192: 8192 + c.S]
        ctv_head = CarB[:, 10240: 10240 + c.S]
        xtok = [Dar[:, 16384 + i * c.D: 16384 + (i + 1) * c.D] for i in range(2)]
        ctv = [sbuf(f"ctv{i}", [128, 128], BF16) for i in range(2)]
        ctv2 = [sbuf(f"ctv2{i}", [128, 128], BF16) for i in range(2)]
        scb = sbuf("scb", [128, 12], F32)
        epsc = sbuf("epsc", [128, 1], F32)

        ps_w = [psumt(f"ps_w{i}", [128, c.W]) for i in range(4)]
        ps_s = [psumt(f"ps_s{i}", [128, c.W]) for i in range(2)]
        PSF = ps_w + ps_s
        ps_t = [psumt(f"ps_t{i}", [128, 1024], BF16) for i in range(2)]

        LD = Sem(es, nc, "LD")
        GD = Sem(es, nc, "GD")
        TS = Sem(es, nc, "TS")
        VS = Sem(es, nc, "VS")
        SS = Sem(es, nc, "SS")
        CC = Sem(es, nc, "CC")
        ST = Sem(es, nc, "ST")
        CONSTS = Sem(es, nc, "CONSTS")
        XT = Sem(es, nc, "XT")
        WF = [Sem(es, nc, f"WF{i}") for i in range(2)]
        XSS = [Sem(es, nc, f"XSS{i}") for i in range(2)]
        ABFS = [Sem(es, nc, f"ABFS{i}") for i in range(2)]
        WP8L = Sem(es, nc, "WP8L")
        CTXL = Sem(es, nc, "CTXL")
        A8L = Sem(es, nc, "A8L")
        CTV2S = [Sem(es, nc, f"CTV2S{i}") for i in range(2)]
        R1S = [Sem(es, nc, f"R1S{i}") for i in range(2)]
        ABS = [Sem(es, nc, f"ABS{i}") for i in range(4)]
        A8S = [Sem(es, nc, f"A8S{i}") for i in range(2)]
        OUTS = [Sem(es, nc, f"OUTS{i}") for i in range(2)]
        BCS = Sem(es, nc, "BCS")
        W8L = [Sem(es, nc, f"W8L{i}") for i in range(2)]
        WCS = [Sem(es, nc, f"WCS{i}") for i in range(4)]
        g_wcv = [Guard() for _ in range(4)]
        WVL = Sem(es, nc, "WVL")
        QKL = [Sem(es, nc, f"QKL{i}") for i in range(2)]
        g_wvb = Guard()
        blk = es.enter_context(nc.Block())

        P = B()

        g_wf = [Guard(), Guard()]
        g_wb = [Guard(), Guard()]
        g_w8b = [Guard(), Guard()]
        g_xt = Guard()
        g_xtok = [Guard(), Guard()]
        g_psw = [Guard() for _ in range(4)]
        g_pss = [Guard(), Guard()]
        g_psf = g_psw + g_pss
        psrot = [0]
        def pick_bank():
            b = psrot[0] % 6
            psrot[0] += 1
            return b
        g_pst = [Guard(), Guard()]
        g_E = [Guard(), Guard()]
        g_ctv = [Guard(), Guard()]
        g_ctv2 = [Guard(), Guard()]
        g_evb = [Guard() for _ in range(4)]
        g_abuf = [Guard() for _ in range(4)]
        g_abf = [Guard(), Guard()]
        g_a8t = [Guard(), Guard()]
        g_xs = [Guard(), Guard()]
        g_mnegb = Guard()
        g_r1t = [Guard(), Guard()]
        g_outt = [Guard(), Guard()]

        P.dma("sync", CONSTS, idb[:, :], idb_h[:, :])
        P.dma("sync", CONSTS, idf[:, :], idf_h[:, :])
        P.dma("sync", CONSTS, ones1[:, :], ones_h[:, :])
        P.dma("sync", CONSTS, mk1[:, :], mk1_h[:, :])
        P.dma("sync", CONSTS, mk2[:, :], mk2_h[:, :])
        P.wait("tensor", CONSTS, CONSTS.n)
        P.opq("vector", lambda e: e.memset(xmax[:, :], 0.0))
        P.opq("vector", lambda e: e.memset(wmax[:, :], 0.0))
        P.opq("vector", lambda e: e.memset(hmax[:, :], 0.0))
        P.opq("vector", lambda e: e.memset(zrow[:, :], 0.0))
        P.opq("vector", lambda e: e.memset(mcols[:, :], 0.0))
        P.op("vector", VS, lambda e: e.memset(epsc[:, :], 1e-5))

        wcv_jobs = []
        for dc in range(c.ND):
            wcv_jobs.append((wq_h[dc, :, :], wq16_s[dc, :, :], SLABW, c.SCALE))
            wcv_jobs.append((wk_h[dc, :, :], wk16_s[dc, :, :], SLABW, 1.0))
        for g2 in range(c.NG):
            for q2 in range(c.ND * 512 // 2048):
                wcv_jobs.append((wv_h[g2, :, 2048 * q2:2048 * (q2 + 1)],
                                 wv16_s[g2, :, 2048 * q2:2048 * (q2 + 1)], 2048, 1.0))
        for dc in range(c.ND):
            wcv_jobs.append((wo_h[128 * dc:128 * (dc + 1), :],
                             wo16_s[128 * dc:128 * (dc + 1), :], c.D, 1.0))

        def emit_cv(k2):
            srcw, dstw, wdt, fold = wcv_jobs[k2]
            par = k2 % 2
            wpar = k2 % 4
            g_wf[par].wait(P, "sync")
            P.dma("sync", WF[par], wf[par][:, 0:wdt], srcw)
            eng = "scalar" if k2 % 2 == 0 else "vector"
            sem = SS if eng == "scalar" else VS
            P.wait(eng, WF[par])
            g_wcv[wpar].wait(P, eng)
            if eng == "scalar":
                P.op(eng, sem, lambda e, wpar=wpar, wdt=wdt, fold=fold: e.activation(
                    wb[wpar][:, 0:wdt], wf[wpar % 2][:, 0:wdt], AF.Copy, scale=fold))
            else:
                if fold == 1.0:
                    P.op(eng, sem, lambda e, wpar=wpar, wdt=wdt: e.tensor_copy(
                        wb[wpar][:, 0:wdt], wf[wpar % 2][:, 0:wdt]))
                else:
                    P.op(eng, sem, lambda e, wpar=wpar, wdt=wdt, fold=fold: e.tensor_scalar(
                        wb[wpar][:, 0:wdt], wf[wpar % 2][:, 0:wdt], fold, None,
                        AluOpType.mult))
            g_wf[par].set((sem, sem.n))
            P.wait("gpsimd", sem)
            P.dma("gpsimd", WCS[wpar], dstw, wb[wpar][:, 0:wdt])
            g_wcv[wpar].set((WCS[wpar], WCS[wpar].n))

        def emit_wsl(k):
            par = k % 2
            g_wf[par].wait(P, "sync")
            P.dma("sync", WF[par], wf[par][:, 0:c.WSL], wsl_h[k, :, :])
            P.wait("vector", WF[par])
            col = 0 if k < c.n_wsl_fc else 1
            P.op("vector", VS, lambda e, par=par: e.reduce_max(
                stat[:, 5:6], wf[par][:, 0:c.WSL], axis=AX.X,
                apply_absolute_value=True))
            P.sw("vector", VS)
            P.op("vector", VS, lambda e, col=col: e.tensor_tensor(
                wmax[:, col:col + 1], wmax[:, col:col + 1], stat[:, 5:6],
                AluOpType.max))
            g_wf[par].set((VS, VS.n))

        P.sw("vector", VS)

        g_xt2 = [Guard(), Guard()]
        for i in range(c.NT):
            par = i % 2
            xt = xt2[par]
            g_xt2[par].wait(P, "sync")
            P.dma("sync", XT, xt[:, 0:c.D], x_h[128 * i:128 * (i + 1), :])
            P.wait("vector", XT)
            P.op("vector", VS, lambda e, i=i, xt=xt: e.reduce_sum(
                stat[:, 0:1] if False else mnegs2[:, i:i + 1], xt[:, 0:c.D], axis=AX.X))
            P.wait("scalar", XT)
            P.op("scalar", SS, lambda e, i=i, xt=xt: e.activation(
                sqD[:, 0:c.D], xt[:, 0:c.D], AF.Square,
                accum_out=m_sb2[:, i:i + 1]))
            g_xt2[par].set((VS, VS.n), (SS, SS.n))
            for k2 in range(4 * i, min(4 * i + 4, len(wcv_jobs))):
                emit_cv(k2)
        for k2 in range(4 * c.NT, len(wcv_jobs)):
            emit_cv(k2)
        P.sw("vector", VS)
        P.wait("vector", SS)
        P.op("vector", VS, lambda e: e.tensor_scalar(
            mnegs2[:, 0:c.NT], mnegs2[:, 0:c.NT], -1.0 / c.D, None, AluOpType.mult))
        P.sw("vector", VS)
        P.op("vector", VS, lambda e: e.tensor_tensor(
            stat[:, 0:c.NT] if False else sqD[:, 0:c.NT], mnegs2[:, 0:c.NT],
            mnegs2[:, 0:c.NT], AluOpType.mult))
        P.op("vector", VS, lambda e: e.tensor_scalar(
            m_sb2[:, 0:c.NT], m_sb2[:, 0:c.NT], 1.0 / c.D, None, AluOpType.mult))
        P.sw("vector", VS)
        P.op("vector", VS, lambda e: e.tensor_tensor(
            m_sb2[:, 0:c.NT], m_sb2[:, 0:c.NT], sqD[:, 0:c.NT],
            AluOpType.subtract))
        P.sw("vector", VS)
        P.wait("scalar", VS)
        P.op("scalar", SS, lambda e: e.activation(
            rstd_sb[:, 0:c.NT], m_sb2[:, 0:c.NT], AF.Sqrt, bias=epsc[:, 0:1]))
        P.wait("vector", SS)
        P.op("vector", VS, lambda e: e.reciprocal(
            m_sb2[:, 0:c.NT], rstd_sb[:, 0:c.NT]))
        P.sw("vector", VS)
        vs_stats = VS.n
        for i in range(c.NT):
            par = i % 2
            xt = xt2[par]
            g_xt2[par].wait(P, "sync")
            P.dma("sync", XT, xt[:, 0:c.D], x_h[128 * i:128 * (i + 1), :])
            P.wait("vector", XT)
            if i == 0:
                P.wait("vector", VS, vs_stats)
            g_xtok[par].wait(P, "vector")
            P.op("vector", VS, lambda e, par=par, i=i, xt=xt: e.tensor_scalar(
                xtok[par][:, :], xt[:, 0:c.D], mnegs2[:, i:i + 1],
                m_sb2[:, i:i + 1], AluOpType.add, AluOpType.mult))
            g_xt2[par].set((VS, VS.n))
            P.op("vector", VS, lambda e, par=par: e.reduce_max(
                stat[:, 5:6], xtok[par][:, :], axis=AX.X,
                apply_absolute_value=True))
            P.sw("vector", VS)
            P.op("vector", VS, lambda e: e.tensor_tensor(
                xmax[:, :], xmax[:, :], stat[:, 5:6], AluOpType.max))
            P.wait("tensor", VS)
            for dc in range(c.ND):
                tp = dc % 2
                g_pst[tp].wait(P, "tensor")
                P.op("tensor", TS, lambda e, par=par, dc=dc, tp=tp: e.transpose(
                    ps_t[tp][:, 0:128], xtok[par][:, 128 * dc:128 * (dc + 1)],
                    idb[:, :]))
                P.wait("scalar", TS)
                P.op("scalar", SS, lambda e, i=i, dc=dc, tp=tp: e.copy(
                    xn[dc][:, 128 * i:128 * (i + 1)], ps_t[tp][:, 0:128]))
                g_pst[tp].set((SS, SS.n))
            g_xtok[par].set((TS, TS.n))
            if i < c.n_wsl:
                emit_wsl(i)
        for k in range(c.NT, c.n_wsl):
            emit_wsl(k)
        ss_xn_done = SS.n

        P.op("vector", VS, lambda e: e.tensor_copy(mcols[:, 0:1], xmax[:, :]))
        P.op("vector", VS, lambda e: e.tensor_copy(mcols[:, 1:3], wmax[:, :]))
        P.wait("tensor", VS)
        g_psf[4].wait(P, "tensor")
        P.op("tensor", TS, lambda e: e.transpose(
            ps_s[0][0:4, 0:128], mcols[:, 0:4], idf[:, :]))
        P.wait("vector", TS)
        P.op("vector", VS, lambda e: e.reduce_max(
            tmp4[0:4, 0:1], ps_s[0][0:4, 0:128], axis=AX.X))
        g_psf[4].set((VS, VS.n))
        vs_cc1 = VS.n

        if DEBUG:
            P.wait("gpsimd", SS, ss_xn_done)
            P.dma("gpsimd", GD, dbg_xn[:, :], xn[0][:, :])
        P.wait("gpsimd", VS, vs_cc1)
        if DEBUG:
            P.dma("gpsimd", GD, dbg_t4[:, :], tmp4[:, :])
        P.dma("gpsimd", BCS, cc1_i[:, :], zrow[:, :])
        P.wait("gpsimd", BCS)
        P.dma("gpsimd", BCS, cc1_i[0:4, 0:1], tmp4[0:4, 0:1], slow=True)
        P.wait("gpsimd", BCS)
        P.op("gpsimd", CC, lambda e: e.collective_compute(
            "AllReduce", AluOpType.max,
            replica_groups=[list(range(c.n_cores))],
            ins=[cc1_i.opt()], outs=[cc1_o.opt()]))
        P.wait("gpsimd", CC)
        for j in range(3):
            P.dma("gpsimd", BCS, scb[:, j:j + 1],
                  cc1_o[j:j + 1, 0:1].partition_broadcast(128))
        bcs_bcast1 = BCS.n
        if DEBUG:
            P.wait("gpsimd", BCS, bcs_bcast1)
            P.dma("gpsimd", GD, dbg_scb1[:, :], scb[:, 0:4])

        P.wait("vector", BCS, bcs_bcast1)
        for (src, dstc) in ((0, 4), (1, 8), (2, 9)):
            P.op("vector", VS, lambda e, src=src: e.tensor_scalar(
                scb[:, 10:11], scb[:, src:src + 1], EPS_CLAMP, None, AluOpType.max))
            P.sw("vector", VS)
            P.op("vector", VS, lambda e: e.reciprocal(scb[:, 11:12], scb[:, 10:11]))
            P.sw("vector", VS)
            P.op("vector", VS, lambda e, dstc=dstc: e.tensor_scalar(
                scb[:, dstc:dstc + 1], scb[:, 11:12], FP8MAX / 2.0, None,
                AluOpType.mult))
        P.sw("vector", VS)
        P.op("vector", VS, lambda e: e.tensor_scalar(
            scb[:, 10:11], scb[:, 0:1], EPS_CLAMP, None, AluOpType.max))
        P.op("vector", VS, lambda e: e.tensor_scalar(
            scb[:, 11:12], scb[:, 1:2], EPS_CLAMP, None, AluOpType.max))
        P.sw("vector", VS)
        P.op("vector", VS, lambda e: e.tensor_tensor(
            scb[:, 5:6], scb[:, 10:11], scb[:, 11:12], AluOpType.mult))
        P.sw("vector", VS)
        P.op("vector", VS, lambda e: e.tensor_scalar(
            scb[:, 5:6], scb[:, 5:6], 4.0 / (FP8MAX * FP8MAX), None, AluOpType.mult))
        vs_consts1 = VS.n

        vg4 = vg.rearrange("p (t h v) -> p t h v", h=4, v=VCOL)
        ts_group_end = 0
        P.wait("sync", SS, ss_xn_done)

        for g in range(c.NG):
            if g == 0:
                for i2 in range(4):
                    P.wait("sync", WCS[i2], WCS[i2].n)
            g_wvb.wait(P, "sync")
            P.dma("sync", WVL, wvb[:, :], wv16_s[g, :, :])
            P.wait("vector", TS, ts_group_end)
            P.op("vector", VS, lambda e: e.memset(vg[:, :], 1.0))
            vs_vg_ones = VS.n


            for qk, (dst, src16) in enumerate(((qg, wq16_s), (kg, wk16_s))):
                for ei in range(4):
                    ec = 4 * g + ei
                    par = ei % 2
                    g_wb[par].wait(P, "sync")
                    P.dma("sync", QKL[par], wb[par][:, 0:SLABW], src16[ec, :, :])
                    P.wait("tensor", QKL[par])
                    banks = [pick_bank() for _ in range(c.NTW)]
                    for tt in range(c.NTW):
                        g_psf[banks[tt]].wait(P, "tensor")
                    for dc in range(c.ND):
                        for tt in range(c.NTW):
                            last = (dc == c.ND - 1 and tt == c.NTW - 1)
                            fn = lambda e, par=par, dc=dc, tt=tt, bk=banks[tt]: e.matmul(
                                PSF[bk][:, :], wb[par][:, 128 * dc:128 * (dc + 1)],
                                xn[dc][:, c.W * tt:c.W * (tt + 1)],
                                start=(dc == 0), stop=(dc == c.ND - 1))
                            if last:
                                P.op("tensor", TS, fn)
                            else:
                                P.opq("tensor", fn)
                    g_wb[par].set((TS, TS.n))
                    P.wait("scalar", TS)
                    for tt in range(c.NTW):
                        P.op("scalar", SS, lambda e, dst=dst, ei=ei, tt=tt, bk=banks[tt]: e.copy(
                            dst[ei][:, c.W * tt:c.W * (tt + 1)], PSF[bk][:, :]))
                        g_psf[banks[tt]].set((SS, SS.n))


            P.wait("tensor", VS, vs_vg_ones)
            P.wait("tensor", WVL, WVL.n)
            for tc in range(c.NT):
                bk = pick_bank()
                g_psf[bk].wait(P, "tensor")
                for dc in range(c.ND):
                    last = (dc == c.ND - 1)
                    fn = lambda e, bk=bk, tc=tc, dc=dc: e.matmul(
                        PSF[bk][:, :], xn[dc][:, 128 * tc:128 * (tc + 1)],
                        wvb[:, 512 * dc:512 * (dc + 1)],
                        start=(dc == 0), stop=(dc == c.ND - 1))
                    if last:
                        P.op("tensor", TS, fn)
                    else:
                        P.opq("tensor", fn)
                P.wait("vector", TS)
                P.op("vector", VS, lambda e, bk=bk, tc=tc: e.tensor_copy(
                    vg4[:, tc, :, 0:128],
                    PSF[bk][:, :].rearrange("p (h x) -> p h x", h=4)))
                g_psf[bk].set((VS, VS.n))
                if tc == c.NT - 1:
                    g_wvb.set((TS, TS.n))
            if DEBUG and g == 0:
                P.wait("gpsimd", VS)
                P.wait("gpsimd", SS)
                P.dma("gpsimd", GD, dbg_q[:, :], qg[0][:, :])
                P.dma("gpsimd", GD, dbg_k[:, :], kg[0][:, :])
                P.dma("gpsimd", GD, dbg_v[:, :], vg[:, 0:528])

            for hi in range(4):
                h = 4 * g + hi

                nqh = c.NJ // c.H
                for q in range(nqh):
                    jb = h * nqh + q
                    par2 = jb % 2
                    g_wf[par2].wait(P, "sync")
                    P.dma("sync", WF[par2], wf[par2][:, 0:SLABW], wfc_h[jb, :, :])
                    P.wait("scalar", WF[par2])
                    if jb == 0:
                        P.wait("scalar", VS, vs_consts1)
                    g_a8t[jb % 2].wait(P, "scalar")
                    P.op("scalar", SS, lambda e, par2=par2, jb=jb: e.activation(
                        a8t[jb % 2][:, 0:SLABW], wf[par2][:, 0:SLABW], AF.Copy,
                        scale=scb[:, 8:9]))
                    g_wf[par2].set((SS, SS.n))
                    P.wait("gpsimd", SS)
                    P.dma("gpsimd", A8S[jb % 2], wfc8_s[jb, :, :],
                          a8t[jb % 2][:, 0:SLABW])
                    g_a8t[jb % 2].set((A8S[jb % 2], A8S[jb % 2].n))

                for j in range(c.NT):
                    ntk = j // 4 + 1
                    for T in range(ntk):
                        bk = pick_bank()
                        diag = (T == j // 4)
                        g_psf[bk].wait(P, "tensor")
                        if diag:
                            P.opq("tensor", lambda e, hi=hi, j=j, T=T, bk=bk: e.matmul(
                                PSF[bk][:, :], qg[hi][:, 128 * j:128 * (j + 1)],
                                kg[hi][:, c.W * T:c.W * (T + 1)],
                                start=True, stop=False))
                            jm = j % 4
                            P.op("tensor", TS, lambda e, bk=bk, jm=jm: e.matmul(
                                PSF[bk][:, :], idb[:, :],
                                mk1[:, 384 - 128 * jm: 384 - 128 * jm + 512],
                                start=False, stop=True))
                        else:
                            P.op("tensor", TS, lambda e, hi=hi, j=j, T=T, bk=bk: e.matmul(
                                PSF[bk][:, :], qg[hi][:, 128 * j:128 * (j + 1)],
                                kg[hi][:, c.W * T:c.W * (T + 1)],
                                start=True, stop=True))
                        P.wait("vector", TS)
                        P.op("vector", VS, lambda e, bk=bk, T=T: e.reduce_max(
                            m_sb[:, T:T + 1], PSF[bk][:, :], axis=AX.X))
                        g_psf[bk].set((VS, VS.n))
                    P.sw("vector", VS)
                    P.op("vector", VS, lambda e, ntk=ntk, j=j: e.reduce_max(
                        mnegs[:, j:j + 1], m_sb[:, 0:ntk], axis=AX.X, negate=True))
                P.sw("vector", VS)
                g_mnegb.wait(P, "vector")
                P.op("vector", VS, lambda e: e.tensor_copy(
                    mnegb16[:, 0:c.NT], mnegs[:, 0:c.NT]))
                P.wait("tensor", VS)
                for j in range(c.NT):
                    tp = j % 2
                    g_pst[tp].wait(P, "tensor")
                    P.op("tensor", TS, lambda e, tp=tp, j=j: e.transpose(
                        ps_t[tp][0:1, 0:128], mnegb16[:, j:j + 1], idb[:, :]))
                    if j == c.NT - 1:
                        g_mnegb.set((TS, TS.n))
                    P.wait("scalar", TS)
                    P.op("scalar", SS, lambda e, j=j, tp=tp: e.copy(
                        mrow[0:1, 128 * j:128 * (j + 1)], ps_t[tp][0:1, 0:128]))
                    g_pst[tp].set((SS, SS.n))
                ss_mrow = SS.n
                if DEBUG and h == 0:
                    P.wait("gpsimd", SS, ss_mrow)
                    P.dma("gpsimd", GD, dbg_mrow[:, :], mrow[0:1, :])

                P.wait("tensor", SS, ss_mrow)
                for J in range(c.NTW):
                    nblk = 4 * J + 4
                    ss_exp = {}

                    def emit_sT(i):
                        par = i % 2
                        pbt = PSF[4 + par]
                        d = i - 4 * J
                        g_psf[4 + par].wait(P, "tensor")
                        P.opq("tensor", lambda e, hi=hi, i=i, J=J, pbt=pbt: e.matmul(
                            pbt[:, :], kg[hi][:, 128 * i:128 * (i + 1)],
                            qg[hi][:, c.W * J:c.W * (J + 1)],
                            start=True, stop=False))
                        if d >= 0:
                            P.opq("tensor", lambda e, J=J, pbt=pbt: e.matmul(
                                pbt[:, :], ones1[:, :],
                                mrow[0:1, c.W * J:c.W * (J + 1)],
                                start=False, stop=False))
                            P.op("tensor", TS, lambda e, pbt=pbt, d=d: e.matmul(
                                pbt[:, :], idb[:, :],
                                mk2[:, 384 - 128 * d: 384 - 128 * d + 512],
                                start=False, stop=True))
                        else:
                            P.op("tensor", TS, lambda e, J=J, pbt=pbt: e.matmul(
                                pbt[:, :], ones1[:, :],
                                mrow[0:1, c.W * J:c.W * (J + 1)],
                                start=False, stop=True))
                        ts_here = TS.n
                        P.wait("scalar", TS, ts_here)
                        g_E[par].wait(P, "scalar")
                        P.op("scalar", SS, lambda e, par=par, pbt=pbt: e.activation(
                            Ebuf[par][:, :], pbt[:, :], AF.Exp))
                        g_psf[4 + par].set((SS, SS.n))
                        ss_exp[i] = SS.n

                    def emit_ctx(i):
                        par = i % 2
                        P.wait("tensor", SS, ss_exp[i])
                        for q4 in range(4):
                            if i == 0:
                                g_psf[q4].wait(P, "tensor")
                            last = (i == nblk - 1 and q4 == 3)
                            fn = lambda e, par=par, q4=q4, i=i, hi=hi, nblk=nblk: e.matmul(
                                ps_w[q4][:, 0:129],
                                Ebuf[par][:, 128 * q4:128 * (q4 + 1)],
                                vg4[:, i, hi, 0:129],
                                start=(i == 0), stop=(i == nblk - 1))
                            if last:
                                P.op("tensor", TS, fn)
                            else:
                                P.opq("tensor", fn)
                        g_E[par].set((TS, TS.n))

                    for i in range(nblk):
                        emit_sT(i)
                        if i >= 1:
                            emit_ctx(i - 1)
                    emit_ctx(nblk - 1)
                    P.wait("vector", TS)
                    for q4 in range(4):
                        sc6 = 6 if q4 % 2 == 0 else 7
                        P.op("vector", VS, lambda e, q4=q4, sc6=sc6: e.reciprocal(
                            stat[:, sc6:sc6 + 1], ps_w[q4][:, 128:129]))
                        P.sw("vector", VS)
                        P.op("vector", VS, lambda e, q4=q4, J=J, sc6=sc6: e.tensor_scalar(
                            ctv_head[:, 128 * (4 * J + q4):128 * (4 * J + q4 + 1)],
                            ps_w[q4][:, 0:128], stat[:, sc6:sc6 + 1],
                            None, AluOpType.mult))
                        g_psf[q4].set((VS, VS.n))
                P.wait("tensor", VS)
                g_ctv[0].wait(P, "vector") if False else None
                for jj in range(c.NT):
                    cp = jj % 2
                    g_pst[cp].wait(P, "tensor")
                    P.op("tensor", TS, lambda e, cp=cp, jj=jj: e.transpose(
                        ps_t[cp][:, 0:128],
                        ctv_head[:, 128 * jj:128 * (jj + 1)], idb[:, :]))
                    P.wait("scalar", TS)
                    g_ctv2[cp].wait(P, "scalar")
                    P.op("scalar", SS, lambda e, cp=cp: e.copy(
                        ctv2[cp][:, :], ps_t[cp][:, 0:128]))
                    g_pst[cp].set((SS, SS.n))
                    P.wait("gpsimd", SS)
                    P.dma("gpsimd", CTV2S[cp],
                          ctx_s[h, :, 128 * jj:128 * (jj + 1)],
                          ctv2[cp][:, :])
                    g_ctv2[cp].set((CTV2S[cp], CTV2S[cp].n))
            ts_group_end = TS.n
        ts_attn_end = TS.n
        a8s_p2 = [s.n for s in A8S]


        P.wait("scalar", VS, vs_consts1)
        P.wait("scalar", TS, ts_attn_end)
        for dc in range(c.ND):
            P.op("scalar", SS, lambda e, dc=dc: e.activation(
                x8[dc][:, :], xn[dc][:, :], AF.Copy, scale=scb[:, 4:5]))
        ss_x8 = SS.n

        P.wait("sync", SS, ss_x8)
        WOL = Sem(es, nc, "WOL")

        for i2 in range(2):
            P.wait("sync", A8S[i2], a8s_p2[i2])
        P.wait("tensor", SS, ss_x8)
        for jb in range(c.NJ):
            if jb % 4 == 0 and jb // 4 < c.ND:
                dc2 = jb // 4
                P.dma("sync", WOL, wob[dc2][:, :],
                      wo16_s[128 * dc2:128 * (dc2 + 1), :])
            par = jb % 2
            g_w8b[par].wait(P, "sync")
            P.dma("sync", W8L[par], w8b[par][:, 0:SLABW], wfc8_s[jb, :, :])
            P.wait("tensor", W8L[par])
            banks = [pick_bank() for _ in range(c.NTW)]
            for tt in range(c.NTW):
                g_psf[banks[tt]].wait(P, "tensor")
            NDP = c.ND // 2
            for dp in range(NDP):
                wpair = w8b[par][:, 256 * dp:256 * (dp + 1)].rearrange(
                    "p (i m) -> p i m", i=2)
                xpair_full = Bar8[:, 2 * dp * c.S:(2 * dp + 2) * c.S].rearrange(
                    "p (i t) -> p i t", i=2)
                for tt in range(c.NTW):
                    last = (dp == NDP - 1 and tt == c.NTW - 1)
                    fn = lambda e, wp=wpair, xp=xpair_full, dp=dp, tt=tt, bk=banks[tt]: e.matmul(
                        PSF[bk][:, :], wp,
                        xp[:, :, c.W * tt:c.W * (tt + 1)],
                        start=(dp == 0), stop=(dp == NDP - 1),
                        perf_mode=MPM.DoubleRow)
                    if last:
                        P.op("tensor", TS, fn)
                    else:
                        P.opq("tensor", fn)
            g_w8b[par].set((TS, TS.n))
            P.wait("scalar", TS)
            for tt in range(c.NTW):
                g_evb[tt].wait(P, "scalar")
                P.op("scalar", SS, lambda e, tt=tt, bk=banks[tt]: e.activation(
                    evb[tt][:, :], PSF[bk][:, :], AF.Relu, scale=scb[:, 5:6]))
                g_psf[banks[tt]].set((SS, SS.n))
                g_abuf[tt].wait(P, "scalar")
                P.sw("scalar", SS)
                P.op("scalar", SS, lambda e, tt=tt: e.activation(
                    abuf[tt][:, :], evb[tt][:, :], AF.Square))
                ss_sq = SS.n
                P.wait("vector", SS, ss_sq)
                P.op("vector", VS, lambda e, tt=tt: e.reduce_max(
                    stat[:, 8:9], evb[tt][:, :], axis=AX.X))
                P.sw("vector", VS)
                P.op("vector", VS, lambda e: e.tensor_tensor(
                    hmax[:, :], hmax[:, :], stat[:, 8:9], AluOpType.max))
                g_evb[tt].set((VS, VS.n))
                P.wait("gpsimd", SS, ss_sq)
                P.dma("gpsimd", ABS[tt], a_s[128 * jb:128 * (jb + 1),
                                             c.W * tt:c.W * (tt + 1)], abuf[tt][:, :])
                g_abuf[tt].set((ABS[tt], ABS[tt].n))
            if jb < c.n_wpq:
                tpq = c.n_wpq // c.NDW
                ee, off = jb // tpq, (jb % tpq) * c.WSL
                g_wf[par].wait(P, "sync")
                P.dma("sync", WF[par], wf[par][:, 0:c.WSL], wp_h[ee, :, off:off + c.WSL])
                P.wait("scalar", WF[par])
                g_a8t[jb % 2].wait(P, "scalar")
                P.op("scalar", SS, lambda e, par=par, jb=jb: e.activation(
                    a8t[jb % 2][:, 0:c.WSL], wf[par][:, 0:c.WSL], AF.Copy,
                    scale=scb[:, 9:10]))
                g_wf[par].set((SS, SS.n))
                P.wait("gpsimd", SS)
                P.dma("gpsimd", A8S[jb % 2], wp8_s[ee, :, off:off + c.WSL],
                      a8t[jb % 2][:, 0:c.WSL])
                g_a8t[jb % 2].set((A8S[jb % 2], A8S[jb % 2].n))
        vs_fc_end = VS.n
        abs_done = [s.n for s in ABS]

        P.wait("scalar", VS, vs_fc_end)
        P.op("scalar", SS, lambda e: e.activation(
            stat[:, 15:16], hmax[:, :], AF.Square))
        P.wait("vector", SS)
        P.op("vector", VS, lambda e: e.tensor_copy(mcols[:, 0:1], stat[:, 15:16]))
        P.wait("tensor", VS)
        g_psf[4].wait(P, "tensor")
        P.op("tensor", TS, lambda e: e.transpose(
            ps_s[0][0:4, 0:128], mcols[:, 0:4], idf[:, :]))
        P.wait("vector", TS)
        P.op("vector", VS, lambda e: e.reduce_max(
            tmp4[0:4, 0:1], ps_s[0][0:4, 0:128], axis=AX.X))
        g_psf[4].set((VS, VS.n))
        vs_cc2 = VS.n
        P.wait("gpsimd", VS, vs_cc2)
        if DEBUG:
            P.dma("gpsimd", GD, dbg_hm[:, 0:1], hmax[:, :], slow=True)
            P.dma("gpsimd", GD, dbg_hm[:, 1:2], stat[:, 15:16], slow=True)
        P.dma("gpsimd", BCS, cc2_i[:, :], zrow[:, :])
        P.wait("gpsimd", BCS)
        P.dma("gpsimd", BCS, cc2_i[0:4, 0:1], tmp4[0:4, 0:1], slow=True)
        P.wait("gpsimd", BCS)
        P.op("gpsimd", CC, lambda e: e.collective_compute(
            "AllReduce", AluOpType.max,
            replica_groups=[list(range(c.n_cores))],
            ins=[cc2_i.opt()], outs=[cc2_o.opt()]))

        def emit_p6(jc):
            par = jc % 2
            if jc == 0:
                P.wait("scalar", VS, vs_consts2)
                for tt in range(4):
                    P.wait("sync", ABS[tt], abs_done[tt])
            g_abf[par].wait(P, "sync")
            P.dma("sync", ABFS[par], abf[par][:, 0:c.S], a_s[128 * jc:128 * (jc + 1), :])
            P.wait("scalar", ABFS[par])
            g_a8t[par].wait(P, "scalar")
            P.op("scalar", SS, lambda e, par=par: e.activation(
                a8t[par][:, 0:c.S], abf[par][:, 0:c.S], AF.Copy,
                scale=scb[:, 6:7]))
            g_abf[par].set((SS, SS.n))
            P.wait("gpsimd", SS)
            for T in range(c.NTW):
                P.dma("gpsimd", A8S[par], a8_s[T, :, c.W * jc:c.W * (jc + 1)],
                      a8t[par][:, c.W * T:c.W * (T + 1)])
            g_a8t[par].set((A8S[par], A8S[par].n))

        P.wait("sync", VS, vs_fc_end)
        P.wait("tensor", WOL, WOL.n)
        r1s_pass0 = [0, 0]
        ts_wo_pass = 0
        vs_consts2 = None
        for p5 in range(2):
            if p5 == 1:
                P.wait("gpsimd", CC)
                P.dma("gpsimd", BCS, scb[:, 3:4], cc2_o[0:1, 0:1].partition_broadcast(128))
                P.wait("vector", BCS, BCS.n)
                P.op("vector", VS, lambda e: e.tensor_scalar(
                    scb[:, 10:11], scb[:, 3:4], EPS_CLAMP, None, AluOpType.max))
                P.sw("vector", VS)
                P.op("vector", VS, lambda e: e.reciprocal(scb[:, 11:12], scb[:, 10:11]))
                P.sw("vector", VS)
                P.op("vector", VS, lambda e: e.tensor_scalar(
                    scb[:, 6:7], scb[:, 11:12], FP8MAX / 2.0, None, AluOpType.mult))
                P.op("vector", VS, lambda e: e.tensor_scalar(
                    scb[:, 11:12], scb[:, 2:3], EPS_CLAMP, None, AluOpType.max))
                P.sw("vector", VS)
                P.op("vector", VS, lambda e: e.tensor_tensor(
                    scb[:, 7:8], scb[:, 10:11], scb[:, 11:12], AluOpType.mult))
                P.sw("vector", VS)
                P.op("vector", VS, lambda e: e.tensor_scalar(
                    scb[:, 7:8], scb[:, 7:8], 4.0 / (FP8MAX * FP8MAX), None, AluOpType.mult))
                vs_consts2 = VS.n
            for k in range(NH):
                dc = NH * p5 + k
                if p5 == 0 and k == 0:
                    P.wait("gpsimd", TS, ts_attn_end)
                    for cp2 in range(2):
                        P.wait("gpsimd", CTV2S[cp2], CTV2S[cp2].n)
                if p5 == 1 and k == 0:
                    P.wait("gpsimd", TS, ts_wo_pass)
                P.dma("gpsimd", CTXL, ctxh[k][:, :], ctx_s[dc, :, :])
            ctxl_ld = CTXL.n
            if DEBUG and p5 == 0:
                P.wait("gpsimd", CTXL, ctxl_ld)
                P.dma("gpsimd", GD, dbg_ctx[:, :], ctxh[0][:, :])
            P.wait("tensor", CTXL, ctxl_ld)
            if p5 == 1:
                P.wait("sync", R1S[0], r1s_pass0[0])
                P.wait("sync", R1S[1], r1s_pass0[1])
            for tc in range(c.NT):
                if p5 == 1:
                    for jc6 in range(tc * 4, tc * 4 + 4):
                        if jc6 < c.NJ:
                            emit_p6(jc6)
                banks = [pick_bank() for _ in range(c.NDW)]
                for ee in range(c.NDW):
                    g_psf[banks[ee]].wait(P, "tensor")
                for k in range(NH):
                    dc = NH * p5 + k
                    for ee in range(c.NDW):
                        last = (k == NH - 1 and ee == c.NDW - 1)
                        fn = lambda e, k=k, dc=dc, tc=tc, ee=ee, bk=banks[ee]: e.matmul(
                            PSF[bk][:, :], ctxh[k][:, 128 * tc:128 * (tc + 1)],
                            wob[dc][:, c.W * ee:c.W * (ee + 1)],
                            start=(k == 0), stop=(k == NH - 1))
                        if last:
                            P.op("tensor", TS, fn)
                        else:
                            P.opq("tensor", fn)
                ts_wo = TS.n
                for ee in range(c.NDW):
                    par = (tc * c.NDW + ee) % 2
                    g_xs[par].wait(P, "sync")
                    src = x_h if p5 == 0 else r1_s
                    P.dma("sync", XSS[par], xs[par][:, :],
                          src[128 * tc:128 * (tc + 1), c.W * ee:c.W * (ee + 1)])
                    P.wait("vector", XSS[par])
                    P.wait("vector", TS, ts_wo)
                    g_r1t[par].wait(P, "vector")
                    P.op("vector", VS, lambda e, par=par, bk=banks[ee]: e.tensor_tensor(
                        r1t[par][:, :], PSF[bk][:, :], xs[par][:, :],
                        AluOpType.add))
                    g_xs[par].set((VS, VS.n))
                    g_psf[banks[ee]].set((VS, VS.n))
                    P.wait("gpsimd", VS)
                    P.dma("gpsimd", R1S[par], r1_s[128 * tc:128 * (tc + 1),
                                                   c.W * ee:c.W * (ee + 1)],
                          r1t[par][:, :])
                    g_r1t[par].set((R1S[par], R1S[par].n))
            if p5 == 0:
                r1s_pass0 = [s.n for s in R1S]
                ts_wo_pass = TS.n


        ss_a8_end = SS.n
        ts_prev_slab = 0
        for i2 in range(2):
            P.wait("sync", A8S[i2], A8S[i2].n)
            P.wait("sync", R1S[i2], R1S[i2].n)
            P.wait("gpsimd", A8S[i2], A8S[i2].n)
        slab_i = 0
        ts_slab_hist = {}
        for ee in range(c.NDW):
            P.wait("sync", TS, ts_prev_slab)
            if ee == 0:
                P.wait("sync", SS, ss_a8_end)
            P.dma("sync", WP8L, wp8sl[:, :], wp8_s[ee, :, :])
            for T in range(c.NTW):
                sp = slab_i % 2
                P.wait("gpsimd", TS, ts_slab_hist.get(slab_i - 2, 0))
                P.dma("gpsimd", A8L, a8sl[sp][:, :], a8_s[T, :, :])
                a8l_here = A8L.n
                P.wait("tensor", A8L, a8l_here)
                P.wait("tensor", WP8L, WP8L.n)
                a8v = a8sl[sp].rearrange("p (j t) -> p j t", t=c.W)
                wpv = wp8sl.rearrange("p (j t) -> p j t", t=c.W)
                NJP = c.NJ // 2
                for tc in range(4):
                    bk = pick_bank()
                    g_psf[bk].wait(P, "tensor")
                    for jp in range(NJP):
                        last = (jp == NJP - 1)
                        fn = lambda e, bk=bk, jp=jp, tc=tc, a8v=a8v, wpv=wpv: e.matmul(
                            PSF[bk][:, :],
                            a8v[:, 2 * jp:2 * jp + 2, 128 * tc:128 * (tc + 1)],
                            wpv[:, 2 * jp:2 * jp + 2, :],
                            start=(jp == 0), stop=(jp == NJP - 1),
                            perf_mode=MPM.DoubleRow)
                        if last:
                            P.op("tensor", TS, fn)
                        else:
                            P.opq("tensor", fn)
                    ts_mm = TS.n
                    if DEBUG and ee == 0 and T == 0 and tc == 0:
                        P.wait("vector", TS, ts_mm)
                        P.op("vector", VS, lambda e: e.tensor_copy(
                            wf[0][:, 0:512], ps_s[0][:, :]))
                        P.wait("gpsimd", VS)
                        P.dma("gpsimd", GD, dbg_ps[:, :], wf[0][:, 0:512])
                        P.dma("gpsimd", GD, dbg_wpsl[:, :], wp8sl[:, :])
                    row = 128 * (4 * T + tc)
                    par = (T * 4 + tc) % 2
                    g_xs[par].wait(P, "sync")
                    P.dma("sync", XSS[par], xs[par][:, :],
                          r1_s[row:row + 128, c.W * ee:c.W * (ee + 1)])
                    P.wait("vector", XSS[par])
                    if DEBUG and ee == 0 and T == 0 and tc == 0:
                        P.wait("gpsimd", XSS[par], XSS[par].n)
                        P.dma("gpsimd", GD, dbg_xs[:, :], xs[par][:, :])
                    P.wait("vector", TS, ts_mm)
                    g_outt[par].wait(P, "vector")
                    P.op("vector", VS, lambda e, bk=bk, par=par: e.tensor_scalar(
                        outt[par][:, :], PSF[bk][:, :], scb[:, 7:8], None,
                        AluOpType.mult))
                    P.sw("vector", VS)
                    P.op("vector", VS, lambda e, par=par: e.tensor_tensor(
                        outt[par][:, :], outt[par][:, :], xs[par][:, :],
                        AluOpType.add))
                    g_psf[bk].set((VS, VS.n))
                    g_xs[par].set((VS, VS.n))
                    P.wait("gpsimd", VS)
                    P.dma("gpsimd", OUTS[par],
                          out_h[row:row + 128, c.W * ee:c.W * (ee + 1)],
                          outt[par][:, :])
                    g_outt[par].set((OUTS[par], OUTS[par].n))
                ts_slab_hist[slab_i] = TS.n
                slab_i += 1
                ts_prev_slab = TS.n

        if DEBUG:
            P.wait("gpsimd", VS, vs_consts2)
            P.dma("gpsimd", GD, dbg_scb2[:, :], scb[:, :])
            P.wait("gpsimd", ST, ST.n)
            P.dma("gpsimd", GD, dbg_r1[:, :], r1_s[:, :])
            P.dma("gpsimd", GD, dbg_as[:, :], a_s[:, :])
            P.dma("gpsimd", GD, dbg_cs[:, :, :], ctx_s[:, :, :])
            P.dma("gpsimd", GD, dbg_wp8[:, :, :], wp8_s[:, :, :])
            P.dma("gpsimd", GD, dbg_a8[:, :, :], a8_s[:, :, :])
            P.wait("gpsimd", GD, GD.n)
        P.wait("gpsimd", OUTS[0], OUTS[0].n)
        P.wait("gpsimd", OUTS[1], OUTS[1].n)

        @blk.sync
        def _(e):
            for f in P.ops["sync"]:
                f(e)

        @blk.vector
        def _(e):
            for f in P.ops["vector"]:
                f(e)

        @blk.tensor
        def _(e):
            for f in P.ops["tensor"]:
                f(e)

        @blk.scalar
        def _(e):
            for f in P.ops["scalar"]:
                f(e)

        @blk.gpsimd
        def _(e):
            for f in P.ops["gpsimd"]:
                f(e)

    return nc


def _host_blocked(c: Cfg, Wq, Wk, Wv, Wo, w_fc, w_proj):
    f32 = np.float32
    wqT = np.ascontiguousarray(Wq.astype(f32, copy=False).T)
    wkT = np.ascontiguousarray(Wk.astype(f32, copy=False).T)
    wvT = np.ascontiguousarray(Wv.astype(f32, copy=False).T)
    woT = np.ascontiguousarray(Wo.astype(f32, copy=False).T)
    wfcT = np.ascontiguousarray(w_fc.astype(f32, copy=False).T)
    wpT = np.ascontiguousarray(w_proj.astype(f32, copy=False).T)

    def blk_lhsT(WT, PM):
        K, M = WT.shape
        nkc, nmc = K // 128, M // PM
        a = WT.reshape(nkc, 128, nmc, PM).transpose(2, 1, 0, 3)
        return np.ascontiguousarray(a).reshape(nmc, 128, nkc * PM)

    wqTb = blk_lhsT(wqT, 128)
    wkTb = blk_lhsT(wkT, 128)
    wfcTb = blk_lhsT(wfcT, 128)
    a = wvT.reshape(c.ND, 128, c.NG, 512).transpose(2, 1, 0, 3)
    wvTb = np.ascontiguousarray(a).reshape(c.NG, 128, c.ND * 512)
    a = wpT.reshape(c.NJ, 128, c.NDW, c.W).transpose(2, 1, 0, 3)
    wpTb = np.ascontiguousarray(a).reshape(c.NDW, 128, c.NJ * c.W)
    return wqTb, wkTb, wvTb, woT, wfcTb, wpTb


def _host_masks():
    bf = ml_dtypes.bfloat16
    m1 = np.zeros((128, 1024), dtype=np.float32)
    tri_u = np.where(np.arange(128)[None, :] > np.arange(128)[:, None], BIGNEG, 0.0)
    m1[:, 384:512] = tri_u
    m1[:, 512:] = BIGNEG
    m2 = np.zeros((128, 1024), dtype=np.float32)
    tri_l = np.where(np.arange(128)[:, None] > np.arange(128)[None, :], BIGNEG, 0.0)
    m2[:, 0:384] = BIGNEG
    m2[:, 384:512] = tri_l
    return m1.astype(bf), m2.astype(bf)


def make_inputs(c: Cfg, x_full, Wq, Wk, Wv, Wo, w_fc, w_proj):
    bf = ml_dtypes.bfloat16
    wqTb, wkTb, wvTb, woT, wfcTb, wpTb = _host_blocked(c, Wq, Wk, Wv, Wo, w_fc, w_proj)
    m1, m2 = _host_masks()
    identb = np.eye(128, dtype=bf)
    identf = np.eye(128, dtype=np.float32)
    onesb = np.ones((1, 128), dtype=bf)

    fc_flat = np.ascontiguousarray(wfcTb.transpose(1, 0, 2)).reshape(128, -1)
    wp_flat = np.ascontiguousarray(wpTb.transpose(1, 0, 2)).reshape(128, -1)
    n1, n2 = c.n_wsl_fc, c.n_wsl_wp
    in_maps = []
    for core in range(c.n_cores):
        s1 = fc_flat[:, core * n1 * c.WSL:(core + 1) * n1 * c.WSL]
        s1 = np.ascontiguousarray(s1.reshape(128, n1, c.WSL).transpose(1, 0, 2))
        s2 = wp_flat[:, core * n2 * c.WSL:(core + 1) * n2 * c.WSL]
        s2 = np.ascontiguousarray(s2.reshape(128, n2, c.WSL).transpose(1, 0, 2))
        wslice = np.ascontiguousarray(np.concatenate([s1, s2], axis=0))
        in_maps.append({
            "x": np.ascontiguousarray(x_full[core]),
            "wqTb": wqTb, "wkTb": wkTb, "wvTb": wvTb, "woT": woT,
            "wfcTb": wfcTb, "wpTb": wpTb, "wslice": wslice,
            "identb": identb, "identf": identf, "onesb": onesb,
            "mask1": m1, "mask2": m2,
        })
    return in_maps


def kernel(L0_attn_x, L0_attn_Wq, L0_attn_Wk, L0_attn_Wv, L0_attn_Wo,
           L0_ffn_w_fc, L0_ffn_w_proj):
    c = Cfg(S=2048, D=2048, H=16, n_cores=8)
    nc = build_nc(c)
    in_maps = make_inputs(c, np.asarray(L0_attn_x),
                          np.asarray(L0_attn_Wq), np.asarray(L0_attn_Wk),
                          np.asarray(L0_attn_Wv), np.asarray(L0_attn_Wo),
                          np.asarray(L0_ffn_w_fc), np.asarray(L0_ffn_w_proj))
    res = run_bass_kernel_spmd(nc, in_maps, core_ids=list(range(c.n_cores)))
    out = np.stack([np.asarray(r["out"]) for r in res.results], axis=0)
    return out.astype(np.float32, copy=False)
